# version 08
# baseline (speedup 1.0000x reference)
import numpy as np
import concourse.bass as bass
import concourse.mybir as mybir
from concourse.bass_utils import run_bass_kernel_spmd

F32 = mybir.dt.float32
F32R = mybir.dt.float32r
BF16 = mybir.dt.bfloat16
AF = mybir.ActivationFunctionType
ALU = mybir.AluOpType
AX = mybir.AxisListType

B, T, H = 4, 4096, 1024
DI, KD, RI = 2048, 64, 16
TC = 2048
TILE = 512
NTILE = TC // TILE
CHUNK = 128
NCH = TILE // CHUNK
FT = 1024
NFT = TC // FT
I_FFN = 2816
IC_N = I_FFN // 128
IS_FFN = 1408
ISC_N = IS_FFN // 128
NE = 8
HC_N = H // 128
DC_N = DI // 128
EPS = 1e-5

DMA_SEM = {"sync": "d_sync", "scalar": "d_scalar", "gpsimd": "d_gpsimd"}


class Buf:
    __slots__ = ("writer", "readers")

    def __init__(self):
        self.writer = None
        self.readers = []


class Sched:
    def __init__(self):
        self.ops = []

    def add(self, engine, fn, reads=(), writes=(), dma=False):
        tok = len(self.ops)
        deps = set()
        for b in reads:
            if b.writer is not None:
                deps.add(b.writer)
        for b in writes:
            if b.writer is not None:
                deps.add(b.writer)
            deps.update(b.readers)
        sem = DMA_SEM[engine] if dma else engine
        self.ops.append([engine, fn, sorted(deps), 16 if dma else 1, False, sem])
        for b in reads:
            b.readers.append(tok)
        for b in writes:
            b.writer = tok
            b.readers = []
        return tok

    OP_COST = {"tensor": 250, "vector": 650, "scalar": 650, "gpsimd": 900,
               "sync": 100}

    def finalize(self):
        n = len(self.ops)
        succs = [[] for _ in range(n)]
        indeg = [0] * n
        for tok, op in enumerate(self.ops):
            for d in op[2]:
                succs[d].append(tok)
                indeg[tok] += 1
        cost = [2500 if op[3] == 16 else self.OP_COST[op[0]] for op in self.ops]
        prio = [0.0] * n
        for tok in range(n - 1, -1, -1):
            m = 0.0
            for s2 in succs[tok]:
                if prio[s2] > m:
                    m = prio[s2]
            prio[tok] = m + cost[tok]
        import heapq
        heap = [(-prio[t], t) for t in range(n) if indeg[t] == 0]
        heapq.heapify(heap)
        order = []
        while heap:
            _, t = heapq.heappop(heap)
            order.append(t)
            for s2 in succs[t]:
                indeg[s2] -= 1
                if indeg[s2] == 0:
                    heapq.heappush(heap, (-prio[s2], s2))
        assert len(order) == n
        self.order = order
        for op in self.ops:
            if op[3] == 16:
                op[4] = True
            for d in op[2]:
                if self.ops[d][5] != op[5]:
                    self.ops[d][4] = True
        self.cum = [0] * n
        counts = {}
        for tok in order:
            op = self.ops[tok]
            if op[4]:
                counts[op[5]] = counts.get(op[5], 0) + op[3]
            self.cum[tok] = counts.get(op[5], 0)

    def emit(self, name, eng, sems):
        waited = {}
        prev = None
        for tok in self.order:
            engine, fn, deps, inc, watched, sem = self.ops[tok]
            if engine != name:
                continue
            for d in deps:
                ds = self.ops[d][5]
                if ds == sem:
                    continue
                v = self.cum[d]
                if waited.get(ds, 0) >= v:
                    continue
                eng.wait_ge(sems[ds], v)
                waited[ds] = v
            if name == "vector" and prev is not None and prev in deps:
                eng.drain()
            inst = fn(eng)
            if watched:
                inst.then_inc(sems[sem], inc)
            prev = tok


DEBUG_TAPS = []


def build_program():
    nc = bass.Bass("TRN2", target_bir_lowering=False)
    S = Sched()
    bufs = {}

    def B_(name):
        if name not in bufs:
            bufs[name] = Buf()
        return bufs[name]

    def din(name, shape):
        return nc.dram_tensor(name, shape, F32, kind="ExternalInput").ap()

    d_x_main = din("x_main", [H, TC])
    d_x_pre = din("x_pre", [H, TC])
    d_smask = din("smask", [128, 1])
    d_cvec = din("cvec", [128, HC_N])
    d_wmod = din("wmod", [H, 6 * H])
    d_up_w = din("up_w", [DI, H])
    d_gate_w = din("gate_w", [DI, H])
    d_conv_w = din("conv_w", [128, DC_N, 4])
    d_conv_b = din("conv_b", [128, DC_N])
    d_qkgt_w = din("qkgt_w", [4 * KD, DI])
    d_g_b = din("g_b", [KD, 1])
    d_tau_b = din("tau_b", [KD, 1])
    d_ig_w1 = din("ig_w1", [RI, DI])
    d_ig_w2 = din("ig_w2", [DI, RI])
    d_ig_b2 = din("ig_b2", [128, DC_N])
    d_rw = din("rw", [128, DC_N])
    d_nw = din("nw", [128, DC_N])
    d_down_w = din("down_w", [H, DI])
    d_fgate = din("fgate", [128, HC_N, NE])
    d_Wg = nc.dram_tensor("Wg", [NE, H, I_FFN], F32, kind="ExternalInput").ap()
    d_Wu = nc.dram_tensor("Wu", [NE, H, I_FFN], F32, kind="ExternalInput").ap()
    d_Wd = nc.dram_tensor("Wd", [NE, I_FFN, H], F32, kind="ExternalInput").ap()
    d_Sg = din("Sg", [H, IS_FFN])
    d_Su = din("Su", [H, IS_FFN])
    d_Sd = din("Sd", [IS_FFN, H])
    d_out = nc.dram_tensor("out", [H, TC], F32, kind="ExternalOutput").ap()

    s_upT = nc.dram_tensor("s_upT", [H, DI], BF16).ap()
    s_gateT = nc.dram_tensor("s_gateT", [H, DI], BF16).ap()
    s_downT = nc.dram_tensor("s_downT", [DI, H], BF16).ap()
    s_qkgtT = nc.dram_tensor("s_qkgtT", [DI, 4 * KD], BF16).ap()
    s_ig2T = nc.dram_tensor("s_ig2T", [RI, DI], BF16).ap()

    ctx = []

    def sb(name, shape, dtype):
        g = nc.sbuf_tensor(name, shape, dtype)
        t = g.__enter__()
        ctx.append(g)
        return t

    ARENA_N = 45104
    arena = sb("arena", [128, ARENA_N], BF16)
    ar = arena.ap()

    def view(lo, hi, dtype, shape):
        a = ar[:, lo:hi]
        if dtype != BF16:
            a = a.bitcast(dtype)
        if shape is not None and len(shape) == 3:
            a = a.rearrange("p (a b) -> p a b", a=shape[1])
        return a

    h_att = view(0, 4096, BF16, [128, 8, 512])
    x_ext = view(4096, 12336, BF16, [128, 16, 515])
    vt_tm = view(4096, 12288, BF16, [128, 4, 2048])
    shift = view(12336, 20528, BF16, [128, 16, 512])
    sfg = view(20528, 28720, BF16, [128, 16, 512])
    vbuf = view(28720, 36912, BF16, [128, 16, 512])
    o_tm = view(36912, 45104, BF16, [128, 4, 2048])
    x_pre_t = ar[:, 36912:45104].bitcast(F32R).rearrange("p (a b) -> p a b", a=8)
    h_ffn = view(0, 8192, BF16, [128, 8, 1024])
    a_ffn = view(8192, 30720, BF16, [128, 22, 1024])
    cbB = ar[:, 30720:32768].bitcast(F32)
    dt_scr = ar[:, 32768:34816].bitcast(F32)
    c_fm = ar[0:NE, 34816:35840]
    crow = ar[0:1, 35840:36864]
    mod_row = ar[:, 0:12288].bitcast(F32)[0:1, :]
    wmod_b = [ar[:, 12336:12848], ar[:, 12848:13360]]

    B_hatt, B_xext, B_shift, B_sfg, B_v, B_otm = (
        B_("hatt"), B_("xext"), B_("shift"), B_("sfg"), B_("v"), B_("otm"))
    AL_hffn = [B_hatt, B_xext]
    AL_affn = [B_xext, B_shift, B_sfg, B_v]
    AL_cb = [B_v]
    AL_cfm = [B_v, B_otm]
    AL_modrow = [B_hatt, B_xext]
    AL_wmodb = [B_shift]

    SARENA_N = 12464
    sarena = sb("sarena", [128, SARENA_N], BF16)
    sa = sarena.ap()
    wS = sa[:, 0:2048].rearrange("p (a b) -> p a b", a=8)
    wS2 = sa[:, 2048:4096].rearrange("p (a b) -> p a b", a=8)
    wSdA = sa[:, 0:4096].rearrange("p (a b) -> p a b", a=16)
    wSdB = sa[:, 4096:8192].rearrange("p (a b) -> p a b", a=16)
    wSq = sa[:, 4096:8192].rearrange("p (a b) -> p a b", a=16)
    wig2s = sa[0:RI, 8192:10240]
    wgS = sa[:, 0:2048].rearrange("p (s a b) -> p s a b", s=2, a=8)
    wuS = sa[:, 2048:4096].rearrange("p (s a b) -> p s a b", s=2, a=8)
    wdS = sa[:, 8192:12416].rearrange("p (s a b) -> p s a b", s=3, a=11)
    tail_sb = sa[:, 12416:12464].rearrange("p (a b) -> p a b", a=16)
    B_wS, B_wS2 = B_("wS"), B_("wS2")
    AL_wg = [B_wS, B_wS2]

    x_res = sb("x_res", [128, HC_N, TC], F32R)
    qs = sb("qs", [KD, TILE], BF16)
    kn = sb("kn", [KD, TILE], BF16)
    igz = kn[0:RI, 0:TILE]
    g_gate = sb("g_gate", [KD, TILE], F32)
    ebtot = g_gate[0:KD, 500:501]
    bg = sb("bg", [KD, TILE], F32)
    scr_k = sb("scr_k", [KD, TILE], F32)
    scr_k2 = sb("scr_k2", [KD, TILE], F32)
    sqf = sb("sqf", [128, TILE], F32R)

    qhat_d = sb("qhat_d", [KD, CHUNK], BF16)
    khat_d = sb("khat_d", [KD, CHUNK], BF16)
    qhat_1 = sb("qhat_1", [KD, 64], BF16)
    khat_1 = sb("khat_1", [KD, 64], BF16)
    qhat_2 = sb("qhat_2", [KD, CHUNK], BF16)
    khat_2 = sb("khat_2", [KD, CHUNK], BF16)
    qe2 = sb("qe2", [KD, CHUNK], BF16)
    kdec = sb("kdec", [KD, CHUNK], BF16)
    kdec_tm = sb("kdec_tm", [CHUNK, KD], BF16)
    a_t = [sb("a_t0", [128, 128], BF16), sb("a_t1", [128, 128], BF16)]
    s_state = sb("s_state", [KD, DI], F32)
    s_bf = sb("s_bf", [KD, DI], BF16)
    rowa = sb("rowa", [1, TILE], F32R)[0:1, :]
    wig1 = sb("wig1", [128, DC_N, RI], BF16)
    twsrc = [ar[:, 14384:14640].bitcast(F32), ar[:, 14640:14896].bitcast(F32)]
    twdst4 = ar[:, 14896:15152]
    ones_r = sb("ones_r", [128, 1], F32R)
    ones_rows = sb("ones_rows", [NE, 128], F32R)
    ones_cb = sb("ones_cb", [128, 1], BF16)
    ones_rb = sb("ones_rb", [1, 128], BF16)
    ident_f = sb("ident_f", [128, 128], F32)
    ident_b = sb("ident_b", [128, 128], BF16)
    mask32 = sb("mask32", [32, 32], BF16)
    smask_sb = sb("smask_sb", [128, 1], F32)
    conv_w_sb = sb("conv_w_sb", [128, DC_N, 4], F32)
    conv_b_sb = sb("conv_b_sb", [128, DC_N], F32)
    ig_b2_sb = sb("ig_b2_sb", [128, DC_N], F32)
    rw_sb = sb("rw_sb", [128, DC_N], F32)
    nw_sb = sb("nw_sb", [128, DC_N], F32)
    cvec_sb = sb("cvec_sb", [128, HC_N], F32)
    scvec = sb("scvec", [128, HC_N], BF16)
    g_b_sb = sb("g_b_sb", [KD, 1], F32)
    ntau_b = sb("ntau_b", [KD, 1], F32)
    tau_b_sb = sb("tau_b_sb", [KD, 1], F32)
    modT = sb("modT", [128, 48], F32)
    tiny_a = ar[:, 36864:36880].bitcast(F32)
    tiny_b = ar[:, 36880:36896].bitcast(F32)
    rcols = ar[:, 37920:37928].bitcast(F32)
    c_tm = ar[:, 37928:37944].bitcast(F32)
    fgate_sb = ar[:, 37944:38008].rearrange("p (a b) -> p a b", a=HC_N)
    cst_sb = sb("cst_sb", [128, 8], F32)
    m1c = rcols[:, 0:1]
    nm1c = rcols[:, 1:2]
    m2c = rcols[:, 2:3]
    denc = rcols[:, 3:4]

    pt = []
    for i in range(8):
        g = nc.psum_tensor("pb%d" % i, [128, 512], F32)
        pt.append(g.__enter__())
        ctx.append(g)
    PS = [p[:, :] for p in pt]
    PB = [B_("psum%d" % i) for i in range(8)]

    def dma(engine, out_ap, in_ap, reads=(), writes=()):
        return S.add(engine,
                     (lambda e, o=out_ap, i=in_ap: e.dma_start(out=o, in_=i)),
                     reads=reads, writes=writes, dma=True)

    def mm_(out_ap, lhs_ap, rhs_ap, start, stop, reads=(), writes=()):
        return S.add("tensor",
                     (lambda e, o=out_ap, l=lhs_ap, r=rhs_ap, s1=start, s2=stop:
                      e.matmul(o, l, r, start=s1, stop=s2)),
                     reads=reads, writes=writes)

    def tr_(out_ap, in_ap, id_ap, reads=(), writes=()):
        return S.add("tensor",
                     (lambda e, o=out_ap, i=in_ap, d=id_ap: e.transpose(o, i, d)),
                     reads=reads, writes=writes)

    def cp_(out_ap, in_ap, reads=(), writes=(), eng="vector"):
        if eng == "scalar":
            return S.add("scalar",
                         (lambda e, o=out_ap, i=in_ap: e.copy(o, i)),
                         reads=reads, writes=writes)
        return S.add(eng,
                     (lambda e, o=out_ap, i=in_ap: e.tensor_copy(o, i)),
                     reads=reads, writes=writes)

    def tt_(out_ap, a_ap, b_ap, op, reads=(), writes=(), eng="vector"):
        return S.add(eng,
                     (lambda e, o=out_ap, a=a_ap, b=b_ap, p=op:
                      e.tensor_tensor(o, a, b, p)),
                     reads=reads, writes=writes)

    def ts_(out_ap, in_ap, s1, s2, op0, op1, reads=(), writes=(), eng="vector"):
        if s2 is None:
            return S.add(eng,
                         (lambda e, o=out_ap, i=in_ap, a=s1, p=op0:
                          e.tensor_single_scalar(o, i, a, p)),
                         reads=reads, writes=writes)
        return S.add(eng,
                     (lambda e, o=out_ap, i=in_ap, a=s1, b=s2, p0=op0, p1=op1:
                      e.tensor_scalar(o, i, a, b, p0, p1)),
                     reads=reads, writes=writes)

    def stt_(out_ap, in0_ap, scalar, in1_ap, op0, op1, reads=(), writes=(),
             eng="vector"):
        return S.add(eng,
                     (lambda e, o=out_ap, i0=in0_ap, s=scalar, i1=in1_ap,
                      p0=op0, p1=op1: e.scalar_tensor_tensor(o, i0, s, i1, p0, p1)),
                     reads=reads, writes=writes)

    def act_(out_ap, in_ap, func, bias=0.0, scale=1.0, reads=(), writes=()):
        return S.add("scalar",
                     (lambda e, o=out_ap, i=in_ap, f=func, b=bias, s=scale:
                      e.activation(o, i, f, bias=b, scale=s)),
                     reads=list(reads) + [B_("consts")], writes=writes)

    def red_(out_ap, in_ap, op, negate=False, reads=(), writes=()):
        return S.add("vector",
                     (lambda e, o=out_ap, i=in_ap, p=op, n=negate:
                      e.tensor_reduce(o, i, AX.X, p, negate=n)),
                     reads=reads, writes=writes)

    def recip_(out_ap, in_ap, reads=(), writes=()):
        return S.add("vector",
                     (lambda e, o=out_ap, i=in_ap: e.reciprocal(o, i)),
                     reads=reads, writes=writes)

    def scan_(out_ap, d0_ap, d1_ap, reads=(), writes=()):
        return S.add("vector",
                     (lambda e, o=out_ap, a=d0_ap, b=d1_ap:
                      e.tensor_tensor_scan(o, a, b, 0.0, ALU.add, ALU.add)),
                     reads=reads, writes=writes)

    def mset_(ap, v, reads=(), writes=()):
        return S.add("gpsimd", (lambda e, a=ap, x=v: e.memset(a, x)),
                     reads=reads, writes=writes)

    BxT = [B_("x%d" % i) for i in range(NTILE)]
    dma("sync", smask_sb[:], d_smask, writes=[B_("smask")])
    dma("sync", cvec_sb[:], d_cvec, writes=[B_("cvec")])
    dma("sync", conv_w_sb[:], d_conv_w, writes=[B_("convw")])
    dma("sync", conv_b_sb[:], d_conv_b, writes=[B_("convb")])
    dma("sync", ig_b2_sb[:], d_ig_b2, writes=[B_("igb2")])
    dma("sync", rw_sb[:], d_rw, writes=[B_("rw")])
    dma("sync", nw_sb[:], d_nw, writes=[B_("nw")])
    dma("sync", g_b_sb[:], d_g_b, writes=[B_("gb")])
    dma("sync", tau_b_sb[:], d_tau_b, writes=[B_("taub")])
    dma("sync", x_res[:, :, :],
        d_x_main.rearrange("(hc p) t -> p hc t", p=128).bitcast(F32R), writes=BxT)

    _cvals = [EPS, 1.0, 1.0 / H, 1.0 / DI, -0.5, -1.0, 0.125, 0.0]
    for _j, _v in enumerate(_cvals):
        nc.const_aps.register(F32, _v, cst_sb[:, _j:_j + 1])
        mset_(cst_sb[:, _j:_j + 1], _v, writes=[B_("consts")])
    mset_(twsrc[0][:, :], 1.0, writes=[B_("tws0")])
    act_(ones_r[:, :], twsrc[0][:, 0:1], AF.Copy,
         reads=[B_("tws0")], writes=[B_("onesr")])
    act_(ones_rows[:, :], twsrc[0][0:NE, :], AF.Copy,
         reads=[B_("tws0")], writes=[B_("onesrw")])
    act_(ones_cb[:, :], twsrc[0][:, 0:1], AF.Copy,
         reads=[B_("tws0")], writes=[B_("onesrw")])
    act_(ones_rb[:, :], twsrc[0][0:1, :], AF.Copy,
         reads=[B_("tws0")], writes=[B_("onesrw")])
    mset_(ident_f[:], 1.0, writes=[B_("identf")])
    S.add("gpsimd", lambda e: e.affine_select(ident_f[:], ident_f[:], [[1, 128]],
                                              ALU.is_equal, 0.0, base=0,
                                              channel_multiplier=-1),
          reads=[B_("identf")], writes=[B_("identf")])
    mset_(ident_b[:], 1.0, writes=[B_("identb")])
    S.add("gpsimd", lambda e: e.affine_select(ident_b[:], ident_b[:], [[1, 128]],
                                              ALU.is_equal, 0.0, base=0,
                                              channel_multiplier=-1),
          reads=[B_("identb")], writes=[B_("identb")])
    mset_(mask32[:], 1.0, writes=[B_("mask32")])
    S.add("gpsimd", lambda e: e.affine_select(mask32[:], mask32[:], [[1, 32]],
                                              ALU.is_ge, 0.0, base=0,
                                              channel_multiplier=-1),
          reads=[B_("mask32")], writes=[B_("mask32")])
    mset_(a_t[0][:, :], 0.0, writes=[B_("at0")])
    mset_(a_t[1][:, :], 0.0, writes=[B_("at1")])
    mset_(s_state[:, :], 0.0, writes=[B_("S")])
    S.add("vector", lambda e: e.tensor_scalar_mul(ntau_b[:], tau_b_sb[:], -1.0),
          reads=[B_("taub")], writes=[B_("ntaub")])

    act_(scvec[:], cvec_sb[:], AF.Silu, reads=[B_("cvec")], writes=[B_("scvec")])
    B_modrow = B_("modrow")
    for s in range(12):
        slot = s % 2
        for hc in range(HC_N):
            wb = wmod_b[hc % 2]
            Bw = B_("wmod%d" % (hc % 2))
            dma("gpsimd", wb,
                d_wmod[hc * 128:(hc + 1) * 128, s * 512:(s + 1) * 512],
                reads=[Bw] + AL_wmodb, writes=[Bw])
            mm_(PS[slot][0:1, :], scvec[:, hc:hc + 1], wb,
                start=(hc == 0), stop=(hc == HC_N - 1),
                reads=[Bw, B_("scvec"), PB[slot]], writes=[PB[slot]])
        cp_(mod_row[0:1, s * 512:(s + 1) * 512], PS[slot][0:1, :],
            reads=[PB[slot]] + AL_modrow, writes=[B_modrow, PB[slot]])
    for i in range(48):
        slot = i % 2
        tr_(PS[slot][0:128, 0:1], mod_row[0:1, i * 128:(i + 1) * 128],
            ident_f[0:1, 0:1],
            reads=[B_modrow, B_("identf"), PB[slot]], writes=[PB[slot]])
        cp_(modT[:, i:i + 1], PS[slot][0:128, 0:1],
            reads=[PB[slot]], writes=[B_("modT"), PB[slot]])
    for base in (16, 40):
        gs = modT[:, base:base + 8]
        ts_(tiny_a[:, 0:8], gs, 0.0, 2.0, ALU.is_ge, ALU.mult,
            reads=[B_("modT")], writes=[B_("tinya")])
        ts_(tiny_a[:, 0:8], tiny_a[:, 0:8], -1.0, None, ALU.add, None,
            reads=[B_("tinya")], writes=[B_("tinya")])
        act_(tiny_b[:, 0:8], gs, AF.Abs, reads=[B_("modT")], writes=[B_("tinyb")])
        ts_(tiny_b[:, 0:8], tiny_b[:, 0:8], 0.25, None, ALU.max, None,
            reads=[B_("tinyb")], writes=[B_("tinyb")])
        tt_(gs, tiny_a[:, 0:8], tiny_b[:, 0:8], ALU.mult,
            reads=[B_("tinya"), B_("tinyb")], writes=[B_("modT")])
    ts_(modT[:, 8:16], modT[:, 8:16], 1.0, None, ALU.add, None,
        reads=[B_("modT")], writes=[B_("modT")])
    ts_(modT[:, 32:40], modT[:, 32:40], 1.0, None, ALU.add, None,
        reads=[B_("modT")], writes=[B_("modT")])
    if "mod" in DEBUG_TAPS:
        ddm = nc.dram_tensor("dbg_mod", [128, 48], F32, kind="ExternalOutput").ap()
        dma("gpsimd", ddm, modT[:, :], reads=[B_("modT")], writes=[B_("dbgm")])
        ddt = nc.dram_tensor("dbg_tiny", [128, 16], F32, kind="ExternalOutput").ap()
        dma("gpsimd", ddt[:, 0:8], tiny_a[:, 0:8], reads=[B_("tinya")],
            writes=[B_("dbgt")])
        dma("gpsimd", ddt[:, 8:16], tiny_b[:, 0:8], reads=[B_("tinyb")],
            writes=[B_("dbgt")])

    Btws = [B_("tws0"), B_("tws1")]
    Btwd = B_("twd")

    def transpose_weight(d_src, rows, cols, d_dst, scale_col=None, scr_buf=None):
        nr, ncc = (rows + 127) // 128, (cols + 127) // 128
        for j in range(ncc):
            cw = min(128, cols - j * 128)
            for i0 in range(0, nr, 2):
                ni = min(2, nr - i0)
                slot = (j + i0) % 2
                for d in range(ni):
                    i = i0 + d
                    tb = twsrc[d % 2]
                    dma("sync", tb[:, 0:cw],
                        d_src[i * 128:(i + 1) * 128, j * 128:j * 128 + cw],
                        reads=[Btws[d % 2]], writes=[Btws[d % 2]])
                    tr_(PS[slot][0:cw, d * 128:(d + 1) * 128], tb[:, 0:cw],
                        ident_f[:],
                        reads=[Btws[d % 2], B_("identf")] +
                        ([PB[slot]] if d == 0 else []),
                        writes=[PB[slot]] if d == ni - 1 else [])
                if scale_col is not None:
                    col = scale_col(j)
                    S.add("vector",
                          (lambda e, o=twdst4[0:cw, 0:ni * 128],
                           i_=PS[slot][0:cw, 0:ni * 128], c=col:
                           e.tensor_scalar_mul(o, i_, c)),
                          reads=[PB[slot], Btwd, B_("nw")], writes=[Btwd, PB[slot]])
                else:
                    cp_(twdst4[0:cw, 0:ni * 128], PS[slot][0:cw, 0:ni * 128],
                        reads=[PB[slot], Btwd], writes=[Btwd, PB[slot]])
                dma("sync", d_dst[j * 128:j * 128 + cw, i0 * 128:(i0 + ni) * 128],
                    twdst4[0:cw, 0:ni * 128], reads=[Btwd],
                    writes=[Btwd] + ([scr_buf] if scr_buf is not None else []))

    B_scr = {"up": B_("scr_up"), "gate": B_("scr_gate"), "down": B_("scr_down"),
             "qkgt": B_("scr_qkgt"), "ig2": B_("scr_ig2")}
    transpose_weight(d_up_w, DI, H, s_upT, scr_buf=B_scr["up"])
    transpose_weight(d_gate_w, DI, H, s_gateT, scr_buf=B_scr["gate"])
    transpose_weight(d_down_w, H, DI, s_downT,
                     scale_col=lambda j: nw_sb[:, j:j + 1], scr_buf=B_scr["down"])
    transpose_weight(d_qkgt_w, 4 * KD, DI, s_qkgtT, scr_buf=B_scr["qkgt"])
    transpose_weight(d_ig_w2, DI, RI, s_ig2T, scr_buf=B_scr["ig2"])

    for j in range(DC_N):
        slot = j % 2
        tb = twsrc[j % 2]
        dma("sync", tb[0:RI, :], d_ig_w1[:, j * 128:(j + 1) * 128],
            reads=[Btws[j % 2]], writes=[Btws[j % 2]])
        tr_(PS[slot][:, 0:RI], tb[0:RI, :], ident_f[0:RI, 0:RI],
            reads=[Btws[j % 2], B_("identf"), PB[slot]], writes=[PB[slot]])
        cp_(wig1[:, j, :], PS[slot][:, 0:RI],
            reads=[PB[slot], B_("wig1")], writes=[B_("wig1"), PB[slot]])

    S.add("vector", lambda e: e.engine_nop(),
          reads=[Btws[0] if False else B_("tws0"), B_("tws1"), Btwd,
                 B_("tinya"), B_("tinyb"), B_modrow,
                 B_("wmod0"), B_("wmod1"), B_("modT"), B_("wig1"),
                 B_("onesr"), B_("onesrw"), B_("consts")],
          writes=[B_("shift"), B_("otm"), B_("hatt"), B_("xext"), B_("v")])

    def emit_ln(xview, xB, width, t0loc, h_out, h_wbufs, ca_col, s_col,
                stats_bf16=False):
        for sl in range(width // 512):
            c0 = t0loc + sl * 512
            for hc in range(HC_N):
                if stats_bf16:
                    hsl = h_out[:, hc, sl * 512:(sl + 1) * 512]
                    cp_(hsl, xview(hc, c0, 512).bitcast(F32),
                        reads=list(xB) + list(h_wbufs), writes=list(h_wbufs))
                    act_(sqf[:, 0:512], hsl, AF.Square,
                         reads=list(h_wbufs), writes=[B_("sqf")])
                    mm_(PS[4][0:1, :], ones_cb[:], hsl,
                        start=(hc == 0), stop=(hc == HC_N - 1),
                        reads=list(h_wbufs) + [B_("onesrw"), PB[4]], writes=[PB[4]])
                else:
                    act_(sqf[:, 0:512], xview(hc, c0, 512).bitcast(F32), AF.Square,
                         reads=list(xB), writes=[B_("sqf")])
                    mm_(PS[4][0:1, :], ones_r[:], xview(hc, c0, 512),
                        start=(hc == 0), stop=(hc == HC_N - 1),
                        reads=list(xB) + [B_("onesr"), PB[4]], writes=[PB[4]])
                mm_(PS[5][0:1, :], ones_r[:], sqf[:, 0:512],
                    start=(hc == 0), stop=(hc == HC_N - 1),
                    reads=[B_("sqf"), B_("onesr"), PB[5]], writes=[PB[5]])
            act_(rowa, PS[4][0:1, :], AF.Copy, scale=1.0 / H,
                 reads=[PB[4]], writes=[B_("rowa"), PB[4]])
            mm_(PS[6][:, :], ones_rows[0:1, :], rowa, True, True,
                reads=[B_("rowa"), B_("onesrw"), PB[6]], writes=[PB[6]])
            act_(rowa, rowa.bitcast(F32), AF.Square,
                 reads=[B_("rowa")], writes=[B_("rowa")])
            stt_(rowa, PS[5][0:1, :], 1.0 / H, rowa.bitcast(F32),
                 ALU.mult, ALU.subtract,
                 reads=[PB[5], B_("rowa")], writes=[B_("rowa"), PB[5]])
            act_(rowa, rowa.bitcast(F32), AF.Ln, bias=EPS,
                 reads=[B_("rowa")], writes=[B_("rowa")])
            act_(rowa, rowa.bitcast(F32), AF.Exp, scale=-0.5,
                 reads=[B_("rowa")], writes=[B_("rowa")])
            mm_(PS[7][:, :], ones_rows[0:1, :], rowa, True, True,
                reads=[B_("rowa"), B_("onesrw"), PB[7]], writes=[PB[7]])
            for hc in range(HC_N):
                tt_(sqf[:, 0:512], xview(hc, c0, 512).bitcast(F32),
                    PS[6][:, :], ALU.subtract, reads=list(xB) + [PB[6]],
                    writes=[B_("sqf")])
                tt_(sqf[:, 0:512], sqf.ap().bitcast(F32)[:, 0:512],
                    PS[7][:, :], ALU.mult, reads=[B_("sqf"), PB[7]],
                    writes=[B_("sqf")])
                stt_(h_out[:, hc, sl * 512:(sl + 1) * 512],
                     sqf.ap().bitcast(F32)[:, 0:512],
                     modT[:, ca_col + hc:ca_col + hc + 1],
                     modT[:, s_col + hc:s_col + hc + 1].broadcast_to([128, 512]),
                     ALU.mult, ALU.add, reads=[B_("sqf"), B_("modT")],
                     writes=h_wbufs)

    att_flip = [0]

    def attn_stream_buf():
        att_flip[0] ^= 1
        return (wS, B_wS) if att_flip[0] == 1 else (wS2, B_wS2)

    def emit_attention(prefix):
        x_dram = d_x_pre if prefix else d_x_main
        for tile in range(NTILE):
            t0 = tile * TILE
            if prefix:
                dma("sync", x_pre_t[:, :, :],
                    x_dram.rearrange("(hc p) t -> p hc t", p=128)[:, :, t0:t0 + TILE]
                    .bitcast(F32R), reads=[B_otm], writes=[B_otm])
                xB = [B_otm]

                def xview(hc, c0, w, _t0=t0):
                    return x_pre_t[:, hc, c0 - _t0:c0 - _t0 + w]
            else:
                xB = [BxT[tile]]

                def xview(hc, c0, w):
                    return x_res[:, hc, c0:c0 + w]

            dma("sync", wSq[:, :, :],
                s_qkgtT.rearrange("(dc p) c -> p dc c", p=128),
                reads=[B_("wSdB"), B_scr["qkgt"]], writes=[B_("wSdB")])
            ts_(wSq[:, :, 0:KD], wSq[:, :, 0:KD], 0.125, None, ALU.mult, None,
                reads=[B_("wSdB")], writes=[B_("wSdB")])
            dma("sync", wig2s, s_ig2T, reads=[B_("wig2s"), B_scr["ig2"]],
                writes=[B_("wig2s")])

            emit_ln(xview, xB, TILE, t0, h_att, [B_hatt], 8, 0,
                    stats_bf16=prefix)

            if prefix and tile == 0:
                mset_(x_ext[:, :, 0:3], 0.0, reads=[B_xext], writes=[B_xext])
            else:
                cp_(x_ext[:, :, 0:3], tail_sb[:, :, :],
                    reads=[B_xext, B_("tail")], writes=[B_xext])
                if (not prefix) and tile == 0:
                    S.add("vector",
                          lambda e: e.tensor_scalar_mul(x_ext[:, :, 0:3],
                                                        x_ext[:, :, 0:3],
                                                        smask_sb[:, 0:1]),
                          reads=[B_xext, B_("smask")], writes=[B_xext])

            def big_proj(scr, scrb, out_write):
                for g8 in range(8):
                    wbuf, Bw = attn_stream_buf()
                    dma("sync", wbuf[:, :, 0:256],
                        scr.rearrange("(hc p) d -> p hc d", p=128)
                        [:, :, g8 * 256:(g8 + 1) * 256], reads=[Bw, scrb],
                        writes=[Bw])
                    for d in range(2):
                        dc = g8 * 2 + d
                        slot = dc % 4
                        for hc in range(HC_N):
                            mm_(PS[slot][:, :], wbuf[:, hc, d * 128:(d + 1) * 128],
                                h_att[:, hc, :], start=(hc == 0),
                                stop=(hc == HC_N - 1),
                                reads=[Bw, B_hatt, PB[slot]], writes=[PB[slot]])
                        out_write(dc, slot)

            def up_out(dc, slot):
                cp_(x_ext[:, dc, 3:3 + 512], PS[slot][:, :],
                    reads=[PB[slot], B_xext], writes=[B_xext, PB[slot]],
                    eng="vector" if dc % 2 == 0 else "scalar")

            big_proj(s_upT, B_scr["up"], up_out)

            if not prefix:
                def fg_out(dc, slot):
                    cp_(sfg[:, dc, :], PS[slot][:, :],
                        reads=[PB[slot], B_sfg], writes=[B_sfg, PB[slot]],
                        eng="vector" if dc % 2 == 0 else "scalar")
                big_proj(s_gateT, B_scr["gate"], fg_out)

            for dc in range(DC_N):
                S.add("vector",
                      (lambda e, dc=dc: e.tensor_scalar_mul(
                          shift[:, dc, :], x_ext[:, dc, 0:512],
                          conv_w_sb[:, dc, 0:1])),
                      reads=[B_xext, B_("convw"), B_shift], writes=[B_shift])
                for j in range(1, 4):
                    stt_(shift[:, dc, :], x_ext[:, dc, j:j + 512],
                         conv_w_sb[:, dc, j:j + 1], shift[:, dc, :],
                         ALU.mult, ALU.add,
                         reads=[B_xext, B_("convw"), B_shift], writes=[B_shift])
            cp_(tail_sb[:, :, :], x_ext[:, :, TILE:TILE + 3],
                reads=[B_xext, B_("tail")], writes=[B_("tail")])

            for dc in range(DC_N):
                S.add("scalar",
                      (lambda e, dc=dc: e.activation(
                          shift[:, dc, :], shift[:, dc, :], AF.Silu,
                          bias=conv_b_sb[:, dc:dc + 1])),
                      reads=[B_shift, B_("convb")], writes=[B_shift])
            if not prefix:
                for dc in range(DC_N):
                    act_(sfg[:, dc, :], sfg[:, dc, :], AF.Silu,
                         reads=[B_sfg], writes=[B_sfg])

            for dc in range(DC_N):
                mm_(PS[4][0:RI, :], wig1[:, dc, :], x_ext[:, dc, 3:3 + 512],
                    start=(dc == 0), stop=(dc == DC_N - 1),
                    reads=[B_("wig1"), B_xext, PB[4]], writes=[PB[4]])
            cp_(igz, PS[4][0:RI, :], reads=[PB[4], B_("kn")],
                writes=[B_("kn"), PB[4]])
            for dc in range(DC_N):
                slot = 4 + (dc % 2)
                mm_(PS[slot][:, :], wig2s[:, dc * 128:(dc + 1) * 128], igz,
                    True, True, reads=[B_("wig2s"), B_("kn"), PB[slot]],
                    writes=[PB[slot]])
                S.add("scalar",
                      (lambda e, dc=dc, slot=slot: e.activation(
                          vbuf[:, dc, :], PS[slot][:, :], AF.Sigmoid,
                          bias=ig_b2_sb[:, dc:dc + 1])),
                      reads=[PB[slot], B_("igb2"), B_v], writes=[B_v, PB[slot]])
                tt_(vbuf[:, dc, :], vbuf[:, dc, :], x_ext[:, dc, 3:3 + 512], ALU.mult,
                    reads=[B_v, B_xext], writes=[B_v])

            for half, slot in ((0, 6), (1, 7)):
                for dc in range(DC_N):
                    mm_(PS[slot][:, :], wSq[:, dc, half * 128:(half + 1) * 128],
                        shift[:, dc, :], start=(dc == 0), stop=(dc == DC_N - 1),
                        reads=[B_("wSdB"), B_shift, PB[slot]], writes=[PB[slot]])
            if not prefix:
                cp_(qs[:, :], PS[6][0:KD, :], reads=[PB[6]], writes=[B_("qs")])
            cp_(sqf[KD:128, :], PS[6][KD:128, :],
                reads=[PB[6], B_("sqf")], writes=[B_("sqf"), PB[6]])
            dma("sync", scr_k2[:, :], sqf.ap().bitcast(F32)[KD:128, :],
                reads=[B_("sqf"), B_("scrk2")], writes=[B_("scrk2")])
            act_(sqf[0:KD, 0:TILE], scr_k2[:, :], AF.Square,
                 reads=[B_("scrk2"), B_("sqf")], writes=[B_("sqf")])
            mm_(PS[4][0:1, :], ones_r[0:KD, :], sqf[0:KD, 0:TILE], True, True,
                reads=[B_("sqf"), B_("onesr"), PB[4]], writes=[PB[4]])
            act_(rowa, PS[4][0:1, :], AF.Ln, reads=[PB[4]],
                 writes=[B_("rowa"), PB[4]])
            act_(rowa, rowa.bitcast(F32), AF.Exp, scale=-0.5,
                 reads=[B_("rowa")], writes=[B_("rowa")])
            ts_(rowa, rowa.bitcast(F32), 1e12, None, ALU.min, None,
                reads=[B_("rowa")], writes=[B_("rowa")])
            mm_(PS[4][0:KD, :], ones_rows[0:1, 0:KD], rowa, True, True,
                reads=[B_("rowa"), B_("onesrw"), PB[4]], writes=[PB[4]])
            cp_(sqf[0:KD, :], PS[4][0:KD, :],
                reads=[PB[4], B_("sqf")], writes=[B_("sqf"), PB[4]])
            tt_(kn[:, :], scr_k2[:, :], sqf.ap().bitcast(F32)[0:KD, :], ALU.mult,
                reads=[B_("scrk2"), B_("sqf")], writes=[B_("kn")])
            act_(scr_k[:, :], PS[7][0:KD, :], AF.Abs, bias=g_b_sb[:, 0:1],
                 reads=[PB[7], B_("gb")], writes=[B_("scrk")])
            act_(scr_k[:, :], scr_k[:, :], AF.Exp, scale=-1.0,
                 reads=[B_("scrk")], writes=[B_("scrk")])
            act_(scr_k[:, :], scr_k[:, :], AF.Ln, bias=1.0,
                 reads=[B_("scrk")], writes=[B_("scrk")])
            act_(g_gate[:, :], PS[7][0:KD, :], AF.Relu, bias=g_b_sb[:, 0:1],
                 reads=[PB[7], B_("gb")], writes=[B_("gg")])
            tt_(g_gate[:, :], g_gate[:, :], scr_k[:, :], ALU.add,
                reads=[B_("gg"), B_("scrk")], writes=[B_("gg")])
            cp_(sqf[KD:128, :], PS[7][KD:128, :],
                reads=[PB[7], B_("sqf")], writes=[B_("sqf"), PB[7]])
            dma("sync", scr_k[:, :], sqf.ap().bitcast(F32)[KD:128, :],
                reads=[B_("sqf"), B_("scrk")], writes=[B_("scrk")])
            act_(scr_k[:, :], scr_k[:, :], AF.Exp, bias=ntau_b[:, 0:1],
                 scale=-1.0, reads=[B_("scrk"), B_("ntaub")], writes=[B_("scrk")])
            ts_(scr_k[:, :], scr_k[:, :], 1.0, None, ALU.add, None,
                reads=[B_("scrk")], writes=[B_("scrk")])
            recip_(scr_k[:, :], scr_k[:, :], reads=[B_("scrk")], writes=[B_("scrk")])
            stt_(scr_k2[:, :], g_gate[:, :], -1.0, scr_k[:, :], ALU.mult, ALU.mult,
                 reads=[B_("gg"), B_("scrk")], writes=[B_("scrk2")])
            scan_(bg[:, :], scr_k2[:, :],
                  cst_sb[0:KD, 7:8].broadcast_to([KD, TILE]),
                  reads=[B_("scrk2"), B_("consts")], writes=[B_("bg")])
            act_(g_gate[:, :], g_gate[:, :], AF.Ln, reads=[B_("gg")],
                 writes=[B_("gg")])
            tt_(g_gate[:, :], g_gate[:, :], scr_k[:, :], ALU.mult,
                reads=[B_("gg"), B_("scrk")], writes=[B_("gg")])
            act_(g_gate[:, :], g_gate[:, :], AF.Exp, reads=[B_("gg")],
                 writes=[B_("gg")])
            tt_(kn[:, :], kn[:, :], g_gate[:, :], ALU.mult,
                reads=[B_("kn"), B_("gg")], writes=[B_("kn")])

            for c in range(NCH):
                for g4 in range(4):
                    slot = 4 + (g4 % 2)
                    for d in range(4):
                        dc = g4 * 4 + d
                        tr_(PS[slot][:, d * 64:(d + 1) * 64].bitcast(BF16),
                            vbuf[:, dc, c * 128:(c + 1) * 128], ident_b[:],
                            reads=[B_v, B_("identb"), PB[slot]],
                            writes=[PB[slot]])
                    cp_(vt_tm[:, c, g4 * 512:(g4 + 1) * 512],
                        PS[slot][:, 0:256].bitcast(BF16),
                        reads=[PB[slot], B_xext], writes=[B_xext, PB[slot]],
                        eng="vector" if g4 % 2 == 0 else "scalar")

            for c in range(NCH):
                t0c = c * CHUNK
                Rcol = cst_sb[0:KD, 7:8] if c == 0 else bg[:, t0c - 1:t0c]
                ts_(scr_k[:, 0:CHUNK], bg[:, t0c:t0c + CHUNK], Rcol, None,
                    ALU.subtract, None, reads=[B_("bg")], writes=[B_("scrk")])
                act_(scr_k2[:, 0:CHUNK], scr_k[:, 0:CHUNK], AF.Exp,
                     reads=[B_("scrk")], writes=[B_("scrk2")])
                if not prefix:
                    tt_(qe2[:, :], qs[:, t0c:t0c + CHUNK], scr_k2[:, 0:CHUNK],
                        ALU.mult, reads=[B_("qs"), B_("scrk2")], writes=[B_("qe2")])
                ts_(scr_k2[:, 0:1], bg[:, t0c + CHUNK - 1:t0c + CHUNK], Rcol, None,
                    ALU.subtract, None, reads=[B_("bg")], writes=[B_("scrk2")])
                act_(ebtot, scr_k2[:, 0:1], AF.Exp,
                     reads=[B_("scrk2"), B_("gg")], writes=[B_("gg")])
                ts_(scr_k[:, 0:CHUNK], bg[:, t0c:t0c + CHUNK],
                    bg[:, t0c + CHUNK - 1:t0c + CHUNK], None, ALU.subtract, None,
                    reads=[B_("bg")], writes=[B_("scrk")])
                act_(scr_k[:, 0:CHUNK], scr_k[:, 0:CHUNK], AF.Exp, scale=-1.0,
                     reads=[B_("scrk")], writes=[B_("scrk")])
                tt_(kdec[:, :], kn[:, t0c:t0c + CHUNK], scr_k[:, 0:CHUNK], ALU.mult,
                    reads=[B_("kn"), B_("scrk")], writes=[B_("kdec")])
                tr_(PS[4][0:CHUNK, 0:KD // 2].bitcast(BF16), kdec[:, :],
                    ident_b[0:KD, 0:KD],
                    reads=[B_("kdec"), B_("identb"), PB[4]], writes=[PB[4]])
                cp_(kdec_tm[:, :], PS[4][0:CHUNK, 0:KD // 2].bitcast(BF16),
                    reads=[PB[4]], writes=[B_("kdtm"), PB[4]])

                if not prefix:
                    a_sb = a_t[c % 2]
                    B_at = B_("at%d" % (c % 2))
                    for m in range(4):
                        ts_(scr_k[:, 32 * m:32 * m + 32],
                            bg[:, t0c + 32 * m:t0c + 32 * m + 32],
                            bg[:, t0c + 32 * m:t0c + 32 * m + 1], None,
                            ALU.subtract, None,
                            reads=[B_("bg")], writes=[B_("scrk")])
                    act_(scr_k2[:, 0:CHUNK], scr_k[:, 0:CHUNK], AF.Exp,
                         reads=[B_("scrk")], writes=[B_("scrk2")])
                    tt_(qhat_d[:, :], qs[:, t0c:t0c + CHUNK], scr_k2[:, 0:CHUNK],
                        ALU.mult, reads=[B_("qs"), B_("scrk2")], writes=[B_("qhd")])
                    act_(scr_k2[:, 0:CHUNK], scr_k[:, 0:CHUNK], AF.Exp, scale=-1.0,
                         reads=[B_("scrk")], writes=[B_("scrk2")])
                    tt_(khat_d[:, :], kn[:, t0c:t0c + CHUNK], scr_k2[:, 0:CHUNK],
                        ALU.mult, reads=[B_("kn"), B_("scrk2")], writes=[B_("khd")])
                    c63 = bg[:, t0c + 63:t0c + 64]
                    ts_(scr_k[:, 0:64], bg[:, t0c + 64:t0c + 128], c63, None,
                        ALU.subtract, None, reads=[B_("bg")], writes=[B_("scrk")])
                    act_(scr_k[:, 0:64], scr_k[:, 0:64], AF.Exp,
                         reads=[B_("scrk")], writes=[B_("scrk")])
                    tt_(qhat_1[:, :], qs[:, t0c + 64:t0c + 128], scr_k[:, 0:64],
                        ALU.mult, reads=[B_("qs"), B_("scrk")], writes=[B_("qh1")])
                    ts_(scr_k[:, 0:64], bg[:, t0c:t0c + 64], c63, None,
                        ALU.subtract, None, reads=[B_("bg")], writes=[B_("scrk")])
                    act_(scr_k[:, 0:64], scr_k[:, 0:64], AF.Exp, scale=-1.0,
                         reads=[B_("scrk")], writes=[B_("scrk")])
                    tt_(khat_1[:, :], kn[:, t0c:t0c + 64], scr_k[:, 0:64], ALU.mult,
                        reads=[B_("kn"), B_("scrk")], writes=[B_("kh1")])
                    for li, (rq0, rk0, bc) in enumerate(((32, 0, 31), (96, 64, 95))):
                        cc2 = bg[:, t0c + bc:t0c + bc + 1]
                        ts_(scr_k[:, li * 32:li * 32 + 32],
                            bg[:, t0c + rq0:t0c + rq0 + 32], cc2, None,
                            ALU.subtract, None, reads=[B_("bg")],
                            writes=[B_("scrk")])
                        ts_(scr_k[:, 64 + li * 32:64 + li * 32 + 32],
                            bg[:, t0c + rk0:t0c + rk0 + 32], cc2, None,
                            ALU.subtract, None, reads=[B_("bg")],
                            writes=[B_("scrk")])
                    act_(scr_k2[:, 0:64], scr_k[:, 0:64], AF.Exp,
                         reads=[B_("scrk")], writes=[B_("scrk2")])
                    act_(scr_k2[:, 64:128], scr_k[:, 64:128], AF.Exp, scale=-1.0,
                         reads=[B_("scrk")], writes=[B_("scrk2")])
                    tt_(qhat_2[:, 32:64], qs[:, t0c + 32:t0c + 64], scr_k2[:, 0:32],
                        ALU.mult, reads=[B_("qs"), B_("scrk2")], writes=[B_("qh2")])
                    tt_(qhat_2[:, 96:128], qs[:, t0c + 96:t0c + 128],
                        scr_k2[:, 32:64], ALU.mult,
                        reads=[B_("qs"), B_("scrk2")], writes=[B_("qh2")])
                    tt_(khat_2[:, 0:32], kn[:, t0c:t0c + 32], scr_k2[:, 64:96],
                        ALU.mult, reads=[B_("kn"), B_("scrk2")], writes=[B_("kh2")])
                    tt_(khat_2[:, 64:96], kn[:, t0c + 64:t0c + 96],
                        scr_k2[:, 96:128], ALU.mult,
                        reads=[B_("kn"), B_("scrk2")], writes=[B_("kh2")])
                    mm_(PS[0][:, 0:128], khat_d[:, :], qhat_d[:, :], True, True,
                        reads=[B_("khd"), B_("qhd"), PB[0]], writes=[PB[0]])
                    mm_(PS[1][0:64, 0:64], khat_1[:, :], qhat_1[:, :], True, True,
                        reads=[B_("kh1"), B_("qh1"), PB[1]], writes=[PB[1]])
                    mm_(PS[2][:, 0:128], khat_2[:, :], qhat_2[:, :], True, True,
                        reads=[B_("kh2"), B_("qh2"), PB[2]], writes=[PB[2]])
                    for m in range(4):
                        tt_(a_sb[32 * m:32 * m + 32, 32 * m:32 * m + 32],
                            PS[0][32 * m:32 * m + 32, 32 * m:32 * m + 32],
                            mask32[:, :], ALU.mult,
                            reads=[PB[0], B_("mask32"), B_at], writes=[B_at])
                    cp_(a_sb[0:64, 64:128], PS[1][0:64, 0:64],
                        reads=[PB[1], B_at], writes=[B_at, PB[1]])
                    cp_(a_sb[0:32, 32:64], PS[2][0:32, 32:64],
                        reads=[PB[2], B_at], writes=[B_at])
                    cp_(a_sb[64:96, 96:128], PS[2][64:96, 96:128],
                        reads=[PB[2], B_at], writes=[B_at, PB[2]])
                    cp_(s_bf[:, :], s_state[:, :], reads=[B_("S")],
                        writes=[B_("sbf")], eng="scalar")
                    for q in range(4):
                        vc0 = q * 512
                        oslot = 4 + (q % 2)
                        mm_(PS[oslot][:, :], qe2[:, :], s_bf[:, vc0:vc0 + 512],
                            True, False,
                            reads=[B_("qe2"), B_("sbf"), PB[oslot]],
                            writes=[PB[oslot]])
                        mm_(PS[oslot][:, :], a_sb[:, :], vt_tm[:, c, vc0:vc0 + 512],
                            False, True,
                            reads=[B_at, B_xext, PB[oslot]], writes=[PB[oslot]])
                        cp_(o_tm[:, c, vc0:vc0 + 512], PS[oslot][:, :],
                            reads=[PB[oslot], B_otm], writes=[B_otm, PB[oslot]],
                            eng="vector" if q % 2 == 0 else "scalar")
                for q in range(4):
                    vc0 = q * 512
                    uslot = 6 + (q % 2)
                    mm_(PS[uslot][0:KD, :], kdec_tm[:, :], vt_tm[:, c, vc0:vc0 + 512],
                        True, True,
                        reads=[B_("kdtm"), B_xext, PB[uslot]], writes=[PB[uslot]])
                    stt_(s_state[:, vc0:vc0 + 512], s_state[:, vc0:vc0 + 512],
                         ebtot, PS[uslot][0:KD, :], ALU.mult, ALU.add,
                         reads=[B_("S"), B_("gg"), PB[uslot]],
                         writes=[B_("S"), PB[uslot]])

            if not prefix:
                for dc in range(DC_N):
                    slot = 4 + (dc % 2)
                    for cc in range(NCH):
                        tr_(PS[slot][:, cc * 64:(cc + 1) * 64].bitcast(BF16),
                            o_tm[:, cc, dc * 128:(dc + 1) * 128], ident_b[:],
                            reads=[B_otm, B_("identb"), PB[slot]],
                            writes=[PB[slot]])
                    stt_(vbuf[:, dc, :], shift[:, dc, :], rw_sb[:, dc:dc + 1],
                         PS[slot][:, 0:256].bitcast(BF16), ALU.mult, ALU.add,
                         reads=[B_shift, B_("rw"), PB[slot], B_v],
                         writes=[B_v, PB[slot]])
                    tt_(vbuf[:, dc, :], vbuf[:, dc, :], sfg[:, dc, :], ALU.mult,
                        reads=[B_v, B_sfg], writes=[B_v])
                for dc in range(DC_N):
                    act_(sqf[:, 0:512], vbuf[:, dc, :], AF.Square,
                         reads=[B_v, B_("sqf")], writes=[B_("sqf")])
                    mm_(PS[6][0:1, :], ones_r[:], sqf[:, 0:512], start=(dc == 0),
                        stop=(dc == DC_N - 1),
                        reads=[B_("sqf"), B_("onesr"), PB[6]], writes=[PB[6]])
                act_(rowa, PS[6][0:1, :], AF.Ln, bias=EPS, scale=1.0 / DI,
                     reads=[PB[6]], writes=[B_("rowa"), PB[6]])
                act_(rowa, rowa.bitcast(F32), AF.Exp, scale=-0.5,
                     reads=[B_("rowa")], writes=[B_("rowa")])
                mm_(PS[6][:, :], ones_rows[0:1, :], rowa, True, True,
                    reads=[B_("rowa"), B_("onesrw"), PB[6]], writes=[PB[6]])
                for dc in range(DC_N):
                    tt_(shift[:, dc, :], vbuf[:, dc, :], PS[6][:, :], ALU.mult,
                        reads=[B_v, PB[6], B_shift],
                        writes=([B_shift, PB[6]] if dc == DC_N - 1 else [B_shift]))
                for pr in range(4):
                    wv = wSdA if pr % 2 == 0 else wSdB
                    Bw = [B_wS, B_wS2] if pr % 2 == 0 else [B_("wSdB")]
                    dma("sync", wv[:, :, :],
                        s_downT.rearrange("(dc p) h -> p dc h", p=128)
                        [:, :, pr * 256:(pr + 1) * 256],
                        reads=Bw + [B_scr["down"]], writes=Bw)
                    for h2 in range(2):
                        hc = pr * 2 + h2
                        slot = hc % 2
                        for dc in range(DC_N):
                            mm_(PS[slot][:, :],
                                wv[:, dc, h2 * 128:(h2 + 1) * 128],
                                shift[:, dc, :], start=(dc == 0),
                                stop=(dc == DC_N - 1),
                                reads=Bw + [B_shift, PB[slot]], writes=[PB[slot]])
                        stt_(x_res[:, hc, t0:t0 + 512], PS[slot][:, :],
                             modT[:, 16 + hc:17 + hc],
                             x_res[:, hc, t0:t0 + 512].bitcast(F32),
                             ALU.mult, ALU.add,
                             reads=[PB[slot], B_("modT"), BxT[tile]],
                             writes=[BxT[tile], PB[slot]])

    emit_attention(prefix=True)
    S.add("vector", lambda e: e.tensor_scalar_mul(s_state[:, :], s_state[:, :],
                                                  smask_sb[0:KD, 0:1]),
          reads=[B_("S"), B_("smask")], writes=[B_("S")])
    emit_attention(prefix=False)

    if "x1" in DEBUG_TAPS:
        d_dbg = nc.dram_tensor("dbg_x1", [H, TC], F32, kind="ExternalOutput").ap()
        dma("gpsimd", d_dbg.rearrange("(hc p) t -> p hc t", p=128),
            x_res.ap().bitcast(F32), reads=BxT, writes=[B_("dbgd")])
    if "epi" in DEBUG_TAPS:
        for nm, ap_, cols in [("sh", shift.rearrange("p a b -> p (a b)").bitcast(F32), 4096),
                              ("fg", sfg.rearrange("p a b -> p (a b)").bitcast(F32), 4096),
                              ("ot", o_tm.rearrange("p a b -> p (a b)").bitcast(F32), 4096),
                              ("xg", vbuf.rearrange("p a b -> p (a b)").bitcast(F32), 4096),
                              ("vt", vt_tm.rearrange("p a b -> p (a b)").bitcast(F32), 4096),
                              ("ss", s_state[:, :], 2048),
                              ("qs2", qs.ap().bitcast(F32)[:, :], 256),
                              ("kn2", kn.ap().bitcast(F32)[:, :], 256),
                              ("gg2", g_gate[:, :], 512),
                              ("bg2", bg[:, :], 512)]:
            dd = nc.dram_tensor("dbg_" + nm, [ap_.shape[0], cols], F32,
                                kind="ExternalOutput").ap()
            dma("gpsimd", dd, ap_,
                reads=[B_shift, B_sfg, B_otm, B_v, B_xext, B_("S"), B_("qs"),
                       B_("kn"), B_("gg"), B_("bg")],
                writes=[B_("dbge")])
    if "prep" in DEBUG_TAPS:
        d_dbg2 = nc.dram_tensor("dbg_prep", [6 * KD, TILE], F32,
                                kind="ExternalOutput").ap()
        for i_, (nm, ap_) in enumerate([("qs", qs.ap().bitcast(F32)[:, 0:256]),
                                        ("kn", kn.ap().bitcast(F32)[:, 0:256]),
                                        ("gg", g_gate[:, :]),
                                        ("bg", bg[:, :]),
                                        ("sk", scr_k[:, :]),
                                        ("sk2", scr_k2[:, :])]):
            dma("gpsimd", d_dbg2[i_ * KD:(i_ + 1) * KD, 0:ap_.shape[-1]], ap_,
                reads=[B_("qs"), B_("kn"), B_("gg"), B_("bg"), B_("scrk"),
                       B_("scrk2")], writes=[B_("dbgp")])

    B_hffn, B_affn, B_cb, B_dt, B_cfm = (B_("hffn"), B_("affn"), B_("cbBv"),
                                         B_("dtv"), B_("cfm"))
    for ft in range(NFT):
        t0 = ft * FT

        def xview_f(hc, c0, w):
            return x_res[:, hc, c0:c0 + w]

        dma("gpsimd", fgate_sb[:, :, :], d_fgate,
            reads=[B_("fgate"), B_otm, B_v], writes=[B_("fgate")])
        BxF = [BxT[2 * ft], BxT[2 * ft + 1]]
        emit_ln(xview_f, BxF, FT, t0, h_ffn, [B_hffn] + AL_hffn, 32, 24)

        for tc in range(FT // 128):
            for hc in range(HC_N):
                mm_(PS[4][0:128, 0:NE], h_ffn[:, hc, tc * 128:(tc + 1) * 128],
                    fgate_sb[:, hc, :], start=(hc == 0), stop=(hc == HC_N - 1),
                    reads=[B_hffn, B_("fgate"), PB[4]], writes=[PB[4]])
            cp_(tiny_a[:, :], PS[4][0:128, 0:NE], reads=[PB[4]],
                writes=[B_("tinya"), PB[4]])
            red_(m1c, tiny_a[:, :], ALU.max, reads=[B_("tinya")],
                 writes=[B_("m1")])
            red_(nm1c, tiny_a[:, :], ALU.max, negate=True,
                 reads=[B_("tinya")], writes=[B_("nm1")])
            ts_(tiny_b[:, :], tiny_a[:, :], m1c, -1e30, ALU.is_equal,
                ALU.mult, reads=[B_("tinya"), B_("m1")], writes=[B_("tinyb")])
            tt_(tiny_b[:, :], tiny_b[:, :], tiny_a[:, :], ALU.add,
                reads=[B_("tinyb"), B_("tinya")], writes=[B_("tinyb")])
            red_(m2c, tiny_b[:, :], ALU.max, reads=[B_("tinyb")],
                 writes=[B_("m2")])
            tt_(denc, m2c, nm1c, ALU.add,
                reads=[B_("m2"), B_("nm1")], writes=[B_("den")])
            act_(denc, denc, AF.Exp, reads=[B_("den")], writes=[B_("den")])
            ts_(denc, denc, 1.0, None, ALU.add, None,
                reads=[B_("den")], writes=[B_("den")])
            recip_(denc, denc, reads=[B_("den")], writes=[B_("den")])
            act_(c_tm[:, :], tiny_a[:, :], AF.Exp, bias=nm1c,
                 reads=[B_("tinya"), B_("nm1")], writes=[B_("ctm")])
            ts_(tiny_b[:, :], tiny_a[:, :], m2c, None, ALU.is_ge, None,
                reads=[B_("tinya"), B_("m2")], writes=[B_("tinyb")])
            stt_(c_tm[:, :], c_tm[:, :], denc, tiny_b[:, :],
                 ALU.mult, ALU.mult,
                 reads=[B_("ctm"), B_("den"), B_("tinyb")], writes=[B_("ctm")])
            tr_(PS[5][0:NE, 0:128], c_tm[:, :], ident_f[:],
                reads=[B_("ctm"), B_("identf"), PB[5]], writes=[PB[5]])
            cp_(c_fm[:, tc * 128:(tc + 1) * 128], PS[5][0:NE, 0:128],
                reads=[PB[5]] + AL_cfm, writes=[B_cfm, PB[5]])

        for ei in range(NE + 1):
            shared = (ei == NE)
            icn = ISC_N if shared else IC_N
            src_g = d_Sg if shared else d_Wg[ei]
            src_u = d_Su if shared else d_Wu[ei]
            src_d = d_Sd if shared else d_Wd[ei]
            if not shared:
                dma("sync", crow, c_fm[ei:ei + 1, :],
                    reads=[B_cfm, B_("crow")] +
                    ([B_v] if ei == 0 and ft == 0 else []),
                    writes=[B_("crow")])
                for q in range(2):
                    mm_(PS[6 + q][:, :], ones_rb[0:1, :],
                        crow[0:1, q * 512:(q + 1) * 512],
                        True, True,
                        reads=[B_("crow"), B_("onesrw"), PB[6 + q]],
                        writes=[PB[6 + q]])
                    cp_(cbB[:, q * 512:(q + 1) * 512], PS[6 + q][:, :],
                        reads=[PB[6 + q]] + AL_cb, writes=[B_cb, PB[6 + q]])
            for ic in range(icn):
                sl = ic % 2
                Bg, Bu = B_("wgS%d" % sl), B_("wuS%d" % sl)
                dma("gpsimd", wgS[:, sl, :, :],
                    src_g.rearrange("(hc p) i -> p hc i", p=128)
                    [:, :, ic * 128:(ic + 1) * 128],
                    reads=[Bg] + (AL_wg if ic < 2 and ei == 0 and ft == 0 else []),
                    writes=[Bg])
                dma("gpsimd", wuS[:, sl, :, :],
                    src_u.rearrange("(hc p) i -> p hc i", p=128)
                    [:, :, ic * 128:(ic + 1) * 128],
                    reads=[Bu] + (AL_wg if ic < 2 and ei == 0 and ft == 0 else []),
                    writes=[Bu])
                gb0 = 0 if ic % 2 == 0 else 4
                ub0 = 2 if ic % 2 == 0 else 6
                for hc in range(HC_N):
                    for tq in range(2):
                        mm_(PS[gb0 + tq][:, :], wgS[:, sl, hc, :],
                            h_ffn[:, hc, tq * 512:(tq + 1) * 512],
                            start=(hc == 0), stop=(hc == HC_N - 1),
                            reads=[Bg, B_hffn, PB[gb0 + tq]],
                            writes=[PB[gb0 + tq]])
                    for tq in range(2):
                        mm_(PS[ub0 + tq][:, :], wuS[:, sl, hc, :],
                            h_ffn[:, hc, tq * 512:(tq + 1) * 512],
                            start=(hc == 0), stop=(hc == HC_N - 1),
                            reads=[Bu, B_hffn, PB[ub0 + tq]],
                            writes=[PB[ub0 + tq]])
                for tq in range(2):
                    S.add("scalar",
                          (lambda e, ic=ic, tq=tq, gb0=gb0: e.activation(
                              a_ffn[:, ic, tq * 512:(tq + 1) * 512],
                              PS[gb0 + tq][:, :], AF.Silu)),
                          reads=[PB[gb0 + tq], B_affn] +
                          (AL_affn if ic < 2 and ei == 0 and ft == 0 else []),
                          writes=[B_affn, PB[gb0 + tq]])
                    tt_(a_ffn[:, ic, tq * 512:(tq + 1) * 512],
                        a_ffn[:, ic, tq * 512:(tq + 1) * 512],
                        PS[ub0 + tq][:, :], ALU.mult,
                        reads=[B_affn, PB[ub0 + tq]],
                        writes=[B_affn, PB[ub0 + tq]])
            h1 = (icn + 1) // 2
            for hc in range(HC_N):
                sl0 = (2 * hc) % 3
                sl1 = (2 * hc + 1) % 3
                Bd0, Bd1 = B_("wdS%d" % sl0), B_("wdS%d" % sl1)
                dma("gpsimd", wdS[:, sl0, 0:h1, :],
                    src_d.rearrange("(ic p) h -> p ic h", p=128)
                    [:, 0:h1, hc * 128:(hc + 1) * 128],
                    reads=[Bd0] + ([B_("wig2s"), B_("tail")]
                                   if hc < 3 and ei == 0 and ft == 0 else []),
                    writes=[Bd0])
                dma("gpsimd", wdS[:, sl1, 0:icn - h1, :],
                    src_d.rearrange("(ic p) h -> p ic h", p=128)
                    [:, h1:icn, hc * 128:(hc + 1) * 128], reads=[Bd1], writes=[Bd1])
                pd0 = 4 if hc % 2 == 0 else 6
                for ic in range(icn):
                    sl = sl0 if ic < h1 else sl1
                    Bd = Bd0 if ic < h1 else Bd1
                    icl = ic if ic < h1 else ic - h1
                    for tq in range(2):
                        mm_(PS[pd0 + tq][:, :], wdS[:, sl, icl, :],
                            a_ffn[:, ic, tq * 512:(tq + 1) * 512],
                            start=(ic == 0), stop=(ic == icn - 1),
                            reads=[Bd, B_affn, PB[pd0 + tq]],
                            writes=[PB[pd0 + tq]])
                for tq in range(2):
                    xsl = x_res[:, hc, t0 + tq * 512:t0 + (tq + 1) * 512]
                    Bxq = BxT[2 * ft + tq]
                    if shared:
                        stt_(xsl, PS[pd0 + tq][:, :], modT[:, 40 + hc:41 + hc],
                             xsl.bitcast(F32), ALU.mult, ALU.add,
                             reads=[PB[pd0 + tq], B_("modT"), Bxq],
                             writes=[Bxq, PB[pd0 + tq]])
                    else:
                        tt_(dt_scr[:, tq * 512:(tq + 1) * 512], PS[pd0 + tq][:, :],
                            cbB[:, tq * 512:(tq + 1) * 512], ALU.mult,
                            reads=[PB[pd0 + tq], B_cb, B_dt] +
                            (AL_cb if ei == 0 and ft == 0 and hc == 0 else []),
                            writes=[B_dt, PB[pd0 + tq]])
                        stt_(xsl, dt_scr[:, tq * 512:(tq + 1) * 512],
                             modT[:, 40 + hc:41 + hc], xsl.bitcast(F32),
                             ALU.mult, ALU.add,
                             reads=[B_dt, B_("modT"), Bxq], writes=[Bxq])

    dma("gpsimd", d_out.rearrange("(hc p) t -> p hc t", p=128),
        x_res.ap().bitcast(F32), reads=BxT, writes=[B_("outd")])
    S.add("gpsimd", lambda e: e.nop(), reads=[B_("outd")], writes=[])

    S.finalize()
    sem_names = ["sync", "scalar", "vector", "tensor", "gpsimd",
                 "d_sync", "d_scalar", "d_gpsimd"]
    sems = {}
    for sn in sem_names:
        g = nc.semaphore("sem_" + sn)
        sems[sn] = g.__enter__()
        ctx.append(g)
    with nc.Block() as block:
        @block.sync
        def _(eng):
            S.emit("sync", eng, sems)

        @block.scalar
        def _(eng):
            S.emit("scalar", eng, sems)

        @block.vector
        def _(eng):
            S.emit("vector", eng, sems)

        @block.tensor
        def _(eng):
            S.emit("tensor", eng, sems)

        @block.gpsimd
        def _(eng):
            S.emit("gpsimd", eng, sems)

    for g in reversed(ctx):
        g.__exit__(None, None, None)
    return nc


_NC_CACHE = {}


def _get_nc():
    if "nc" not in _NC_CACHE:
        _NC_CACHE["nc"] = build_program()
    return _NC_CACHE["nc"]


def kernel(**inputs):
    hs = np.asarray(inputs["L0_attn_hidden_states"], dtype=np.float32)

    def pf(name):
        return np.ascontiguousarray(np.asarray(inputs[name], dtype=np.float32))

    common = dict(
        up_w=pf("L0_attn_up_w"),
        gate_w=pf("L0_attn_gate_w"),
        conv_w=np.ascontiguousarray(
            pf("L0_attn_conv_w").reshape(DC_N, 128, 4).transpose(1, 0, 2)),
        conv_b=np.ascontiguousarray(pf("L0_attn_conv_b").reshape(DC_N, 128).T),
        qkgt_w=np.ascontiguousarray(np.concatenate(
            [pf("L0_attn_q_w"), pf("L0_attn_k_w"),
             pf("L0_attn_g_w"), pf("L0_attn_tau_w")], axis=0)),
        g_b=np.ascontiguousarray(pf("L0_attn_g_b").reshape(KD, 1)),
        tau_b=np.ascontiguousarray(pf("L0_attn_tau_b").reshape(KD, 1)),
        ig_w1=pf("L0_attn_ig_w1"),
        ig_w2=pf("L0_attn_ig_w2"),
        ig_b2=np.ascontiguousarray(pf("L0_attn_ig_b2").reshape(DC_N, 128).T),
        rw=np.ascontiguousarray(
            pf("L0_attn_residual_weight").reshape(DC_N, 128).T),
        nw=np.ascontiguousarray(pf("L0_attn_norm_weight").reshape(DC_N, 128).T),
        down_w=pf("L0_attn_down_w"),
        fgate=np.ascontiguousarray(
            pf("L0_ffn_gate_w").reshape(HC_N, 128, NE).transpose(1, 0, 2)),
        Wg=pf("L0_ffn_Wg"), Wu=pf("L0_ffn_Wu"), Wd=pf("L0_ffn_Wd"),
        Sg=pf("L0_ffn_Sg"), Su=pf("L0_ffn_Su"), Sd=pf("L0_ffn_Sd"),
        wmod=pf("L0_wmod"),
    )
    c_all = pf("L0_c")

    in_maps = []
    for core in range(8):
        b, hf = core // 2, core % 2
        xm = np.ascontiguousarray(hs[b, hf * TC:(hf + 1) * TC, :].T)
        xp = np.ascontiguousarray(hs[b, 0:TC, :].T) if hf == 1 else xm
        m = dict(common)
        m.update(x_main=xm, x_pre=xp,
                 smask=np.full((128, 1), float(hf), np.float32),
                 cvec=np.ascontiguousarray(c_all[b].reshape(HC_N, 128).T))
        in_maps.append(m)

    nc = _get_nc()
    res = run_bass_kernel_spmd(nc, in_maps, core_ids=list(range(8)))
    out = np.empty((B, T, H), dtype=np.float32)
    for core in range(8):
        b, hf = core // 2, core % 2
        out[b, hf * TC:(hf + 1) * TC, :] = res.results[core]["out"].T
    return out


# version 09 (final)
# speedup vs baseline: 1.0181x; 1.0181x over previous
import numpy as np
import concourse.bass as bass
import concourse.mybir as mybir
from concourse.bass_utils import run_bass_kernel_spmd

F32 = mybir.dt.float32
F32R = mybir.dt.float32r
BF16 = mybir.dt.bfloat16
AF = mybir.ActivationFunctionType
ALU = mybir.AluOpType
AX = mybir.AxisListType

B, T, H = 4, 4096, 1024
DI, KD, RI = 2048, 64, 16
TC = 2048
TILE = 512
NTILE = TC // TILE
CHUNK = 128
NCH = TILE // CHUNK
FT = 1024
NFT = TC // FT
I_FFN = 2816
IC_N = I_FFN // 128
IS_FFN = 1408
ISC_N = IS_FFN // 128
NE = 8
HC_N = H // 128
DC_N = DI // 128
EPS = 1e-5

DMA_SEM = {"sync": "d_sync", "scalar": "d_scalar", "gpsimd": "d_gpsimd"}


class Buf:
    __slots__ = ("writer", "readers")

    def __init__(self):
        self.writer = None
        self.readers = []


class Sched:
    def __init__(self):
        self.ops = []

    def add(self, engine, fn, reads=(), writes=(), dma=False, aps=None):
        tok = len(self.ops)
        import os
        if os.environ.get("SDBG"):
            import traceback
            fs = traceback.extract_stack()
            self.lbl = getattr(self, "lbl", [])
            self.lbl.append(";".join(f"{f.lineno}" for f in fs[-4:-1]))
        self.apl = getattr(self, "apl", [])
        self.apl.append([(a, w) for a, w in (aps or []) if a is not None
                         and not isinstance(a, (int, float))])
        deps = set()
        for b in reads:
            if b.writer is not None:
                deps.add(b.writer)
        for b in writes:
            if b.writer is not None:
                deps.add(b.writer)
            deps.update(b.readers)
        sem = DMA_SEM[engine] if dma else engine
        self.ops.append([engine, fn, sorted(deps), 16 if dma else 1, False, sem])
        for b in reads:
            b.readers.append(tok)
        for b in writes:
            b.writer = tok
            b.readers = []
        return tok

    OP_COST = {"tensor": 250, "vector": 650, "scalar": 650, "gpsimd": 900,
               "sync": 100}

    _ESZ = {"float32": 4, "float32r": 4, "bfloat16": 2, "float16": 2,
            "int32": 4, "uint8": 1, "int8": 1}

    @classmethod
    def _box(cls, ap):
        sp = str(ap.space).split(".")[-1]
        t = ap.tensor
        esz = cls._ESZ[str(ap.dtype).split(".")[-1]]
        shape = tuple(int(s) for s in ap.shape)
        idx = tuple(slice(s - 1, s) for s in shape)
        if sp == "DRAM":
            return (sp, t.name, 0, 1, int(ap.offset) * esz,
                    (int(ap[idx].offset) + 1) * esz)
        tesz = cls._ESZ[str(t.dtype).split(".")[-1]]
        row_bytes = tesz
        for x in list(t.shape)[1:]:
            row_bytes *= int(x)
        bp = (int(ap.offset) * esz) // row_bytes
        np_ = shape[0]
        lo = int(ap.offset) * esz - bp * row_bytes
        hi = int(ap[idx].offset) * esz - (bp + np_ - 1) * row_bytes + esz
        return (sp, t.name, bp, bp + np_, lo, max(hi, lo + esz))

    @staticmethod
    def _boxes_clash(wboxes, boxes):
        for (sp, nm, p0, p1, b0, b1) in wboxes:
            for (sp2, nm2, q0, q1, c0, c1) in boxes:
                if (sp == sp2 and nm == nm2 and p0 < q1 and q0 < p1
                        and b0 < c1 and c0 < b1):
                    return True
        return False

    def finalize(self):
        n = len(self.ops)
        import os
        sfk = int(os.environ.get("SFK", "0"))
        if sfk:
            print(f"SFK={sfk} n_ops={n}", flush=True)
            k = min(sfk, n)
            for tok in range(1, k):
                if tok - 1 not in self.ops[tok][2]:
                    self.ops[tok][2] = sorted(set(self.ops[tok][2]) | {tok - 1})
            for tok in range(k, n):
                self.ops[tok][2] = sorted(set(self.ops[tok][2]) | {k - 1})
        succs = [[] for _ in range(n)]
        indeg = [0] * n
        for tok, op in enumerate(self.ops):
            for d in op[2]:
                succs[d].append(tok)
                indeg[tok] += 1
        cost = [2500 if op[3] == 16 else self.OP_COST[op[0]] for op in self.ops]
        prio = [0.0] * n
        for tok in range(n - 1, -1, -1):
            m = 0.0
            for s2 in succs[tok]:
                if prio[s2] > m:
                    m = prio[s2]
            prio[tok] = m + cost[tok]
        import heapq
        heap = [(-prio[t], t) for t in range(n) if indeg[t] == 0]
        heapq.heapify(heap)
        order = []
        while heap:
            _, t = heapq.heappop(heap)
            order.append(t)
            for s2 in succs[t]:
                indeg[s2] -= 1
                if indeg[s2] == 0:
                    heapq.heappush(heap, (-prio[s2], s2))
        assert len(order) == n
        self.order = order
        for op in self.ops:
            if op[3] == 16:
                op[4] = True
            for d in op[2]:
                if self.ops[d][5] != op[5]:
                    self.ops[d][4] = True
        self.cum = [0] * n
        counts = {}
        for tok in order:
            op = self.ops[tok]
            if op[4]:
                counts[op[5]] = counts.get(op[5], 0) + op[3]
            self.cum[tok] = counts.get(op[5], 0)
        self.vread = [None] * n
        self.vwrit = [None] * n
        for tok in range(n):
            if self.ops[tok][0] != "vector":
                continue
            rb, wb = [], []
            ok = True
            for ap, isw in self.apl[tok]:
                try:
                    b = self._box(ap)
                except Exception:
                    ok = False
                    break
                rb.append(b)
                if isw:
                    wb.append(b)
            if ok:
                self.vread[tok] = rb
                self.vwrit[tok] = wb

    def emit(self, name, eng, sems):
        import os
        nofifo = bool(os.environ.get("SNOFIFO"))
        waited = {}
        prev = None
        for tok in self.order:
            engine, fn, deps, inc, watched, sem = self.ops[tok]
            if engine != name:
                continue
            for d in deps:
                ds = self.ops[d][5]
                if ds == sem and not (nofifo and inc == 16):
                    continue
                v = self.cum[d]
                if waited.get(ds, 0) >= v:
                    continue
                eng.wait_ge(sems[ds], v)
                waited[ds] = v
            if name == "vector" and prev is not None:
                pw, pa = self.vwrit[prev], self.vread[prev]
                cw, ca = self.vwrit[tok], self.vread[tok]
                if os.environ.get("SDRAIN"):
                    eng.drain()
                elif None in (pw, pa, cw, ca):
                    eng.drain()
                elif self._boxes_clash(pw, ca) or self._boxes_clash(cw, pa):
                    eng.drain()
            inst = fn(eng)
            if watched:
                inst.then_inc(sems[sem], inc)
            prev = tok


DEBUG_TAPS = []


def build_program():
    nc = bass.Bass("TRN2", target_bir_lowering=False)
    S = Sched()
    bufs = {}

    def B_(name):
        if name not in bufs:
            bufs[name] = Buf()
        return bufs[name]

    def din(name, shape):
        return nc.dram_tensor(name, shape, F32, kind="ExternalInput").ap()

    d_x_main = din("x_main", [H, TC])
    d_x_pre = din("x_pre", [H, TC])
    d_smask = din("smask", [128, 1])
    d_cvec = din("cvec", [128, HC_N])
    d_wmod = din("wmod", [H, 6 * H])
    d_up_w = din("up_w", [DI, H])
    d_gate_w = din("gate_w", [DI, H])
    d_conv_w = din("conv_w", [128, DC_N, 4])
    d_conv_b = din("conv_b", [128, DC_N])
    d_qkgt_w = din("qkgt_w", [4 * KD, DI])
    d_g_b = din("g_b", [KD, 1])
    d_tau_b = din("tau_b", [KD, 1])
    d_ig_w1 = din("ig_w1", [RI, DI])
    d_ig_w2 = din("ig_w2", [DI, RI])
    d_ig_b2 = din("ig_b2", [128, DC_N])
    d_rw = din("rw", [128, DC_N])
    d_nw = din("nw", [128, DC_N])
    d_down_w = din("down_w", [H, DI])
    d_fgate = din("fgate", [128, HC_N, NE])
    d_Wg = nc.dram_tensor("Wg", [NE, H, I_FFN], F32, kind="ExternalInput").ap()
    d_Wu = nc.dram_tensor("Wu", [NE, H, I_FFN], F32, kind="ExternalInput").ap()
    d_Wd = nc.dram_tensor("Wd", [NE, I_FFN, H], F32, kind="ExternalInput").ap()
    d_Sg = din("Sg", [H, IS_FFN])
    d_Su = din("Su", [H, IS_FFN])
    d_Sd = din("Sd", [IS_FFN, H])
    d_out = nc.dram_tensor("out", [H, TC], F32, kind="ExternalOutput").ap()

    s_upT = nc.dram_tensor("s_upT", [H, DI], BF16).ap()
    s_gateT = nc.dram_tensor("s_gateT", [H, DI], BF16).ap()
    s_downT = nc.dram_tensor("s_downT", [DI, H], BF16).ap()
    s_qkgtT = nc.dram_tensor("s_qkgtT", [DI, 4 * KD], BF16).ap()
    s_ig2T = nc.dram_tensor("s_ig2T", [RI, DI], BF16).ap()

    ctx = []

    def sb(name, shape, dtype):
        g = nc.sbuf_tensor(name, shape, dtype)
        t = g.__enter__()
        ctx.append(g)
        return t

    ARENA_N = 45104
    arena = sb("arena", [128, ARENA_N], BF16)
    ar = arena.ap()

    def view(lo, hi, dtype, shape):
        a = ar[:, lo:hi]
        if dtype != BF16:
            a = a.bitcast(dtype)
        if shape is not None and len(shape) == 3:
            a = a.rearrange("p (a b) -> p a b", a=shape[1])
        return a

    h_att = view(0, 4096, BF16, [128, 8, 512])
    x_ext = view(4096, 12336, BF16, [128, 16, 515])
    vt_tm = view(4096, 12288, BF16, [128, 4, 2048])
    shift = view(12336, 20528, BF16, [128, 16, 512])
    sfg = view(20528, 28720, BF16, [128, 16, 512])
    vbuf = view(28720, 36912, BF16, [128, 16, 512])
    o_tm = view(36912, 45104, BF16, [128, 4, 2048])
    x_pre_t = ar[:, 36912:45104].bitcast(F32R).rearrange("p (a b) -> p a b", a=8)
    h_ffn = view(0, 8192, BF16, [128, 8, 1024])
    a_ffn = view(8192, 30720, BF16, [128, 22, 1024])
    cbB = ar[:, 30720:32768].bitcast(F32)
    dt_scr = ar[:, 32768:34816].bitcast(F32)
    c_fm = ar[0:NE, 34816:35840]
    crow = ar[0:1, 35840:36864]
    mod_row = ar[:, 0:12288].bitcast(F32)[0:1, :]
    wmod_b = [ar[:, 12336:12848], ar[:, 12848:13360]]

    B_hatt, B_xext, B_shift, B_sfg, B_v, B_otm = (
        B_("hatt"), B_("xext"), B_("shift"), B_("sfg"), B_("v"), B_("otm"))
    AL_hffn = [B_hatt, B_xext]
    AL_affn = [B_xext, B_shift, B_sfg, B_v]
    AL_cb = [B_v]
    AL_cfm = [B_v, B_otm]
    AL_modrow = [B_hatt, B_xext]
    AL_wmodb = [B_shift]

    SARENA_N = 12464
    sarena = sb("sarena", [128, SARENA_N], BF16)
    sa = sarena.ap()
    wS = sa[:, 0:2048].rearrange("p (a b) -> p a b", a=8)
    wS2 = sa[:, 2048:4096].rearrange("p (a b) -> p a b", a=8)
    wSdA = sa[:, 0:4096].rearrange("p (a b) -> p a b", a=16)
    wSdB = sa[:, 4096:8192].rearrange("p (a b) -> p a b", a=16)
    wSq = sa[:, 4096:8192].rearrange("p (a b) -> p a b", a=16)
    wig2s = sa[0:RI, 8192:10240]
    wgS = sa[:, 0:2048].rearrange("p (s a b) -> p s a b", s=2, a=8)
    wuS = sa[:, 2048:4096].rearrange("p (s a b) -> p s a b", s=2, a=8)
    wdS = sa[:, 8192:12416].rearrange("p (s a b) -> p s a b", s=3, a=11)
    tail_sb = sa[:, 12416:12464].rearrange("p (a b) -> p a b", a=16)
    B_wS, B_wS2 = B_("wS"), B_("wS2")
    AL_wg = [B_wS, B_wS2]

    x_res = sb("x_res", [128, HC_N, TC], F32R)
    qs = sb("qs", [KD, TILE], BF16)
    kn = sb("kn", [KD, TILE], BF16)
    igz = kn[0:RI, 0:TILE]
    g_gate = sb("g_gate", [KD, TILE], F32)
    ebtot = g_gate[0:KD, 500:501]
    bg = sb("bg", [KD, TILE], F32)
    scr_k = sb("scr_k", [KD, TILE], F32)
    scr_k2 = sb("scr_k2", [KD, TILE], F32)
    sqf = sb("sqf", [128, TILE], F32R)

    qhat_d = sb("qhat_d", [KD, CHUNK], BF16)
    khat_d = sb("khat_d", [KD, CHUNK], BF16)
    qhat_1 = sb("qhat_1", [KD, 64], BF16)
    khat_1 = sb("khat_1", [KD, 64], BF16)
    qhat_2 = sb("qhat_2", [KD, CHUNK], BF16)
    khat_2 = sb("khat_2", [KD, CHUNK], BF16)
    qe2 = sb("qe2", [KD, CHUNK], BF16)
    kdec = sb("kdec", [KD, CHUNK], BF16)
    kdec_tm = sb("kdec_tm", [CHUNK, KD], BF16)
    a_t = [sb("a_t0", [128, 128], BF16), sb("a_t1", [128, 128], BF16)]
    s_state = sb("s_state", [KD, DI], F32)
    s_bf = sb("s_bf", [KD, DI], BF16)
    rowa = sb("rowa", [1, TILE], F32R)[0:1, :]
    wig1 = sb("wig1", [128, DC_N, RI], BF16)
    twsrc = [ar[:, 14384:14640].bitcast(F32), ar[:, 14640:14896].bitcast(F32)]
    twdst4 = ar[:, 14896:15152]
    ones_r = sb("ones_r", [128, 1], F32R)
    ones_rows = sb("ones_rows", [NE, 128], F32R)
    ones_cb = sb("ones_cb", [128, 1], BF16)
    ones_rb = sb("ones_rb", [1, 128], BF16)
    ident_f = sb("ident_f", [128, 128], F32)
    ident_b = sb("ident_b", [128, 128], BF16)
    mask32 = sb("mask32", [32, 32], BF16)
    smask_sb = sb("smask_sb", [128, 1], F32)
    conv_w_sb = sb("conv_w_sb", [128, DC_N, 4], F32)
    conv_b_sb = sb("conv_b_sb", [128, DC_N], F32)
    ig_b2_sb = sb("ig_b2_sb", [128, DC_N], F32)
    rw_sb = sb("rw_sb", [128, DC_N], F32)
    nw_sb = sb("nw_sb", [128, DC_N], F32)
    cvec_sb = sb("cvec_sb", [128, HC_N], F32)
    scvec = sb("scvec", [128, HC_N], BF16)
    g_b_sb = sb("g_b_sb", [KD, 1], F32)
    ntau_b = sb("ntau_b", [KD, 1], F32)
    tau_b_sb = sb("tau_b_sb", [KD, 1], F32)
    modT = sb("modT", [128, 48], F32)
    tiny_a = ar[:, 36864:36880].bitcast(F32)
    tiny_b = ar[:, 36880:36896].bitcast(F32)
    rcols = ar[:, 37920:37928].bitcast(F32)
    c_tm = ar[:, 37928:37944].bitcast(F32)
    fgate_sb = ar[:, 37944:38008].rearrange("p (a b) -> p a b", a=HC_N)
    cst_sb = sb("cst_sb", [128, 8], F32)
    m1c = rcols[:, 0:1]
    nm1c = rcols[:, 1:2]
    m2c = rcols[:, 2:3]
    denc = rcols[:, 3:4]

    pt = []
    for i in range(8):
        g = nc.psum_tensor("pb%d" % i, [128, 512], F32)
        pt.append(g.__enter__())
        ctx.append(g)
    PS = [p[:, :] for p in pt]
    PB = [B_("psum%d" % i) for i in range(8)]

    def dma(engine, out_ap, in_ap, reads=(), writes=()):
        return S.add(engine,
                     (lambda e, o=out_ap, i=in_ap: e.dma_start(out=o, in_=i)),
                     reads=reads, writes=writes, dma=True,
                     aps=[(out_ap, True), (in_ap, False)])

    def mm_(out_ap, lhs_ap, rhs_ap, start, stop, reads=(), writes=()):
        return S.add("tensor",
                     (lambda e, o=out_ap, l=lhs_ap, r=rhs_ap, s1=start, s2=stop:
                      e.matmul(o, l, r, start=s1, stop=s2)),
                     reads=reads, writes=writes,
                     aps=[(out_ap, True), (lhs_ap, False), (rhs_ap, False)])

    def tr_(out_ap, in_ap, id_ap, reads=(), writes=()):
        return S.add("tensor",
                     (lambda e, o=out_ap, i=in_ap, d=id_ap: e.transpose(o, i, d)),
                     reads=reads, writes=writes,
                     aps=[(out_ap, True), (in_ap, False), (id_ap, False)])

    def cp_(out_ap, in_ap, reads=(), writes=(), eng="vector"):
        if eng == "scalar":
            return S.add("scalar",
                         (lambda e, o=out_ap, i=in_ap: e.copy(o, i)),
                         reads=reads, writes=writes,
                         aps=[(out_ap, True), (in_ap, False)])
        return S.add(eng,
                     (lambda e, o=out_ap, i=in_ap: e.tensor_copy(o, i)),
                     reads=reads, writes=writes,
                     aps=[(out_ap, True), (in_ap, False)])

    def tt_(out_ap, a_ap, b_ap, op, reads=(), writes=(), eng="vector"):
        return S.add(eng,
                     (lambda e, o=out_ap, a=a_ap, b=b_ap, p=op:
                      e.tensor_tensor(o, a, b, p)),
                     reads=reads, writes=writes,
                     aps=[(out_ap, True), (a_ap, False), (b_ap, False)])

    def ts_(out_ap, in_ap, s1, s2, op0, op1, reads=(), writes=(), eng="vector"):
        if s2 is None:
            return S.add(eng,
                         (lambda e, o=out_ap, i=in_ap, a=s1, p=op0:
                          e.tensor_single_scalar(o, i, a, p)),
                         reads=reads, writes=writes,
                         aps=[(out_ap, True), (in_ap, False), (s1, False)])
        return S.add(eng,
                     (lambda e, o=out_ap, i=in_ap, a=s1, b=s2, p0=op0, p1=op1:
                      e.tensor_scalar(o, i, a, b, p0, p1)),
                     reads=reads, writes=writes,
                     aps=[(out_ap, True), (in_ap, False), (s1, False),
                          (s2, False)])

    def stt_(out_ap, in0_ap, scalar, in1_ap, op0, op1, reads=(), writes=(),
             eng="vector"):
        return S.add(eng,
                     (lambda e, o=out_ap, i0=in0_ap, s=scalar, i1=in1_ap,
                      p0=op0, p1=op1: e.scalar_tensor_tensor(o, i0, s, i1, p0, p1)),
                     reads=reads, writes=writes,
                     aps=[(out_ap, True), (in0_ap, False), (scalar, False),
                          (in1_ap, False)])

    def act_(out_ap, in_ap, func, bias=0.0, scale=1.0, reads=(), writes=()):
        return S.add("scalar",
                     (lambda e, o=out_ap, i=in_ap, f=func, b=bias, s=scale:
                      e.activation(o, i, f, bias=b, scale=s)),
                     reads=list(reads) + [B_("consts")], writes=writes,
                     aps=[(out_ap, True), (in_ap, False), (bias, False)])

    def red_(out_ap, in_ap, op, negate=False, reads=(), writes=()):
        return S.add("vector",
                     (lambda e, o=out_ap, i=in_ap, p=op, n=negate:
                      e.tensor_reduce(o, i, AX.X, p, negate=n)),
                     reads=reads, writes=writes,
                     aps=[(out_ap, True), (in_ap, False)])

    def recip_(out_ap, in_ap, reads=(), writes=()):
        return S.add("vector",
                     (lambda e, o=out_ap, i=in_ap: e.reciprocal(o, i)),
                     reads=reads, writes=writes,
                     aps=[(out_ap, True), (in_ap, False)])

    def scan_(out_ap, d0_ap, d1_ap, reads=(), writes=()):
        return S.add("vector",
                     (lambda e, o=out_ap, a=d0_ap, b=d1_ap:
                      e.tensor_tensor_scan(o, a, b, 0.0, ALU.add, ALU.add)),
                     reads=reads, writes=writes,
                     aps=[(out_ap, True), (d0_ap, False), (d1_ap, False)])

    def mset_(ap, v, reads=(), writes=()):
        return S.add("gpsimd", (lambda e, a=ap, x=v: e.memset(a, x)),
                     reads=reads, writes=writes, aps=[(ap, True)])

    BxT = [B_("x%d" % i) for i in range(NTILE)]
    dma("sync", smask_sb[:], d_smask, writes=[B_("smask")])
    dma("sync", cvec_sb[:], d_cvec, writes=[B_("cvec")])
    dma("sync", conv_w_sb[:], d_conv_w, writes=[B_("convw")])
    dma("sync", conv_b_sb[:], d_conv_b, writes=[B_("convb")])
    dma("sync", ig_b2_sb[:], d_ig_b2, writes=[B_("igb2")])
    dma("sync", rw_sb[:], d_rw, writes=[B_("rw")])
    dma("sync", nw_sb[:], d_nw, writes=[B_("nw")])
    dma("sync", g_b_sb[:], d_g_b, writes=[B_("gb")])
    dma("sync", tau_b_sb[:], d_tau_b, writes=[B_("taub")])
    dma("sync", x_res[:, :, :],
        d_x_main.rearrange("(hc p) t -> p hc t", p=128).bitcast(F32R), writes=BxT)

    _cvals = [EPS, 1.0, 1.0 / H, 1.0 / DI, -0.5, -1.0, 0.125, 0.0]
    for _j, _v in enumerate(_cvals):
        nc.const_aps.register(F32, _v, cst_sb[:, _j:_j + 1])
        mset_(cst_sb[:, _j:_j + 1], _v, writes=[B_("consts")])
    mset_(twsrc[0][:, :], 1.0, writes=[B_("tws0")])
    act_(ones_r[:, :], twsrc[0][:, 0:1], AF.Copy,
         reads=[B_("tws0")], writes=[B_("onesr")])
    act_(ones_rows[:, :], twsrc[0][0:NE, :], AF.Copy,
         reads=[B_("tws0")], writes=[B_("onesrw")])
    act_(ones_cb[:, :], twsrc[0][:, 0:1], AF.Copy,
         reads=[B_("tws0")], writes=[B_("onesrw")])
    act_(ones_rb[:, :], twsrc[0][0:1, :], AF.Copy,
         reads=[B_("tws0")], writes=[B_("onesrw")])
    mset_(ident_f[:], 1.0, writes=[B_("identf")])
    S.add("gpsimd", lambda e: e.affine_select(ident_f[:], ident_f[:], [[1, 128]],
                                              ALU.is_equal, 0.0, base=0,
                                              channel_multiplier=-1),
          reads=[B_("identf")], writes=[B_("identf")],
          aps=[(ident_f[:], True)])
    mset_(ident_b[:], 1.0, writes=[B_("identb")])
    S.add("gpsimd", lambda e: e.affine_select(ident_b[:], ident_b[:], [[1, 128]],
                                              ALU.is_equal, 0.0, base=0,
                                              channel_multiplier=-1),
          reads=[B_("identb")], writes=[B_("identb")],
          aps=[(ident_b[:], True)])
    mset_(mask32[:], 1.0, writes=[B_("mask32")])
    S.add("gpsimd", lambda e: e.affine_select(mask32[:], mask32[:], [[1, 32]],
                                              ALU.is_ge, 0.0, base=0,
                                              channel_multiplier=-1),
          reads=[B_("mask32")], writes=[B_("mask32")],
          aps=[(mask32[:], True)])
    mset_(a_t[0][:, :], 0.0, writes=[B_("at0")])
    mset_(a_t[1][:, :], 0.0, writes=[B_("at1")])
    mset_(s_state[:, :], 0.0, writes=[B_("S")])
    S.add("vector", lambda e: e.tensor_scalar_mul(ntau_b[:], tau_b_sb[:], -1.0),
          reads=[B_("taub")], writes=[B_("ntaub")],
          aps=[(ntau_b[:], True), (tau_b_sb[:], False)])

    act_(scvec[:], cvec_sb[:], AF.Silu, reads=[B_("cvec")], writes=[B_("scvec")])
    B_modrow = B_("modrow")
    for s in range(12):
        slot = s % 2
        for hc in range(HC_N):
            wb = wmod_b[hc % 2]
            Bw = B_("wmod%d" % (hc % 2))
            dma("gpsimd", wb,
                d_wmod[hc * 128:(hc + 1) * 128, s * 512:(s + 1) * 512],
                reads=[Bw], writes=[Bw] + AL_wmodb)
            mm_(PS[slot][0:1, :], scvec[:, hc:hc + 1], wb,
                start=(hc == 0), stop=(hc == HC_N - 1),
                reads=[Bw, B_("scvec"), PB[slot]], writes=[PB[slot]])
        cp_(mod_row[0:1, s * 512:(s + 1) * 512], PS[slot][0:1, :],
            reads=[PB[slot]], writes=[B_modrow, PB[slot]] + AL_modrow)
    for i in range(48):
        slot = i % 2
        tr_(PS[slot][0:128, 0:1], mod_row[0:1, i * 128:(i + 1) * 128],
            ident_f[0:1, 0:1],
            reads=[B_modrow, B_("identf"), PB[slot]], writes=[PB[slot]])
        cp_(modT[:, i:i + 1], PS[slot][0:128, 0:1],
            reads=[PB[slot]], writes=[B_("modT"), PB[slot]])
    for base in (16, 40):
        gs = modT[:, base:base + 8]
        ts_(tiny_a[:, 0:8], gs, 0.0, 2.0, ALU.is_ge, ALU.mult,
            reads=[B_("modT")], writes=[B_("tinya")])
        ts_(tiny_a[:, 0:8], tiny_a[:, 0:8], -1.0, None, ALU.add, None,
            reads=[B_("tinya")], writes=[B_("tinya")])
        act_(tiny_b[:, 0:8], gs, AF.Abs, reads=[B_("modT")], writes=[B_("tinyb")])
        ts_(tiny_b[:, 0:8], tiny_b[:, 0:8], 0.25, None, ALU.max, None,
            reads=[B_("tinyb")], writes=[B_("tinyb")])
        tt_(gs, tiny_a[:, 0:8], tiny_b[:, 0:8], ALU.mult,
            reads=[B_("tinya"), B_("tinyb")], writes=[B_("modT")])
    ts_(modT[:, 8:16], modT[:, 8:16], 1.0, None, ALU.add, None,
        reads=[B_("modT")], writes=[B_("modT")])
    ts_(modT[:, 32:40], modT[:, 32:40], 1.0, None, ALU.add, None,
        reads=[B_("modT")], writes=[B_("modT")])
    if "mod" in DEBUG_TAPS:
        ddm = nc.dram_tensor("dbg_mod", [128, 48], F32, kind="ExternalOutput").ap()
        dma("gpsimd", ddm, modT[:, :], reads=[B_("modT")], writes=[B_("dbgm")])
        ddt = nc.dram_tensor("dbg_tiny", [128, 16], F32, kind="ExternalOutput").ap()
        dma("gpsimd", ddt[:, 0:8], tiny_a[:, 0:8], reads=[B_("tinya")],
            writes=[B_("dbgt")])
        dma("gpsimd", ddt[:, 8:16], tiny_b[:, 0:8], reads=[B_("tinyb")],
            writes=[B_("dbgt")])

    Btws = [B_("tws0"), B_("tws1")]
    Btwd = B_("twd")

    def transpose_weight(d_src, rows, cols, d_dst, scale_col=None, scr_buf=None):
        nr, ncc = (rows + 127) // 128, (cols + 127) // 128
        for j in range(ncc):
            cw = min(128, cols - j * 128)
            for i0 in range(0, nr, 2):
                ni = min(2, nr - i0)
                slot = (j + i0) % 2
                for d in range(ni):
                    i = i0 + d
                    tb = twsrc[d % 2]
                    dma("sync", tb[:, 0:cw],
                        d_src[i * 128:(i + 1) * 128, j * 128:j * 128 + cw],
                        reads=[Btws[d % 2]], writes=[Btws[d % 2]])
                    tr_(PS[slot][0:cw, d * 128:(d + 1) * 128], tb[:, 0:cw],
                        ident_f[:],
                        reads=[Btws[d % 2], B_("identf")] +
                        ([PB[slot]] if d == 0 else []),
                        writes=[PB[slot]] if d == ni - 1 else [])
                if scale_col is not None:
                    col = scale_col(j)
                    S.add("vector",
                          (lambda e, o=twdst4[0:cw, 0:ni * 128],
                           i_=PS[slot][0:cw, 0:ni * 128], c=col:
                           e.tensor_scalar_mul(o, i_, c)),
                          reads=[PB[slot], Btwd, B_("nw")], writes=[Btwd, PB[slot]],
                          aps=[(twdst4[0:cw, 0:ni * 128], True),
                               (PS[slot][0:cw, 0:ni * 128], False), (col, False)])
                else:
                    cp_(twdst4[0:cw, 0:ni * 128], PS[slot][0:cw, 0:ni * 128],
                        reads=[PB[slot], Btwd], writes=[Btwd, PB[slot]])
                dma("sync", d_dst[j * 128:j * 128 + cw, i0 * 128:(i0 + ni) * 128],
                    twdst4[0:cw, 0:ni * 128], reads=[Btwd],
                    writes=[Btwd] + ([scr_buf] if scr_buf is not None else []))

    B_scr = {"up": B_("scr_up"), "gate": B_("scr_gate"), "down": B_("scr_down"),
             "qkgt": B_("scr_qkgt"), "ig2": B_("scr_ig2")}
    transpose_weight(d_up_w, DI, H, s_upT, scr_buf=B_scr["up"])
    transpose_weight(d_gate_w, DI, H, s_gateT, scr_buf=B_scr["gate"])
    transpose_weight(d_down_w, H, DI, s_downT,
                     scale_col=lambda j: nw_sb[:, j:j + 1], scr_buf=B_scr["down"])
    transpose_weight(d_qkgt_w, 4 * KD, DI, s_qkgtT, scr_buf=B_scr["qkgt"])
    transpose_weight(d_ig_w2, DI, RI, s_ig2T, scr_buf=B_scr["ig2"])

    for j in range(DC_N):
        slot = j % 2
        tb = twsrc[j % 2]
        dma("sync", tb[0:RI, :], d_ig_w1[:, j * 128:(j + 1) * 128],
            reads=[Btws[j % 2]], writes=[Btws[j % 2]])
        tr_(PS[slot][:, 0:RI], tb[0:RI, :], ident_f[0:RI, 0:RI],
            reads=[Btws[j % 2], B_("identf"), PB[slot]], writes=[PB[slot]])
        cp_(wig1[:, j, :], PS[slot][:, 0:RI],
            reads=[PB[slot], B_("wig1")], writes=[B_("wig1"), PB[slot]])

    S.add("vector", lambda e: e.engine_nop(),
          reads=[Btws[0] if False else B_("tws0"), B_("tws1"), Btwd,
                 B_("tinya"), B_("tinyb"), B_modrow,
                 B_("wmod0"), B_("wmod1"), B_("modT"), B_("wig1"),
                 B_("onesr"), B_("onesrw"), B_("consts")],
          writes=[B_("shift"), B_("otm"), B_("hatt"), B_("xext"), B_("v")],
          aps=[])

    def emit_ln(xview, xB, width, t0loc, h_out, h_wbufs, ca_col, s_col,
                stats_bf16=False):
        for sl in range(width // 512):
            c0 = t0loc + sl * 512
            for hc in range(HC_N):
                if stats_bf16:
                    hsl = h_out[:, hc, sl * 512:(sl + 1) * 512]
                    cp_(hsl, xview(hc, c0, 512).bitcast(F32),
                        reads=list(xB) + list(h_wbufs), writes=list(h_wbufs))
                    act_(sqf[:, 0:512], hsl, AF.Square,
                         reads=list(h_wbufs), writes=[B_("sqf")])
                    mm_(PS[4][0:1, :], ones_cb[:], hsl,
                        start=(hc == 0), stop=(hc == HC_N - 1),
                        reads=list(h_wbufs) + [B_("onesrw"), PB[4]], writes=[PB[4]])
                else:
                    act_(sqf[:, 0:512], xview(hc, c0, 512).bitcast(F32), AF.Square,
                         reads=list(xB), writes=[B_("sqf")])
                    mm_(PS[4][0:1, :], ones_r[:], xview(hc, c0, 512),
                        start=(hc == 0), stop=(hc == HC_N - 1),
                        reads=list(xB) + [B_("onesr"), PB[4]], writes=[PB[4]])
                mm_(PS[5][0:1, :], ones_r[:], sqf[:, 0:512],
                    start=(hc == 0), stop=(hc == HC_N - 1),
                    reads=[B_("sqf"), B_("onesr"), PB[5]], writes=[PB[5]])
            act_(rowa, PS[4][0:1, :], AF.Copy, scale=1.0 / H,
                 reads=[PB[4]], writes=[B_("rowa"), PB[4]])
            mm_(PS[6][:, :], ones_rows[0:1, :], rowa, True, True,
                reads=[B_("rowa"), B_("onesrw"), PB[6]], writes=[PB[6]])
            act_(rowa, rowa.bitcast(F32), AF.Square,
                 reads=[B_("rowa")], writes=[B_("rowa")])
            stt_(rowa, PS[5][0:1, :], 1.0 / H, rowa.bitcast(F32),
                 ALU.mult, ALU.subtract,
                 reads=[PB[5], B_("rowa")], writes=[B_("rowa"), PB[5]])
            act_(rowa, rowa.bitcast(F32), AF.Ln, bias=EPS,
                 reads=[B_("rowa")], writes=[B_("rowa")])
            act_(rowa, rowa.bitcast(F32), AF.Exp, scale=-0.5,
                 reads=[B_("rowa")], writes=[B_("rowa")])
            mm_(PS[7][:, :], ones_rows[0:1, :], rowa, True, True,
                reads=[B_("rowa"), B_("onesrw"), PB[7]], writes=[PB[7]])
            for hc in range(HC_N):
                tt_(sqf[:, 0:512], xview(hc, c0, 512).bitcast(F32),
                    PS[6][:, :], ALU.subtract, reads=list(xB) + [PB[6]],
                    writes=[B_("sqf")])
                tt_(sqf[:, 0:512], sqf.ap().bitcast(F32)[:, 0:512],
                    PS[7][:, :], ALU.mult, reads=[B_("sqf"), PB[7]],
                    writes=[B_("sqf")])
                stt_(h_out[:, hc, sl * 512:(sl + 1) * 512],
                     sqf.ap().bitcast(F32)[:, 0:512],
                     modT[:, ca_col + hc:ca_col + hc + 1],
                     modT[:, s_col + hc:s_col + hc + 1].broadcast_to([128, 512]),
                     ALU.mult, ALU.add, reads=[B_("sqf"), B_("modT")],
                     writes=h_wbufs)

    att_flip = [0]

    def attn_stream_buf():
        att_flip[0] ^= 1
        return (wS, B_wS) if att_flip[0] == 1 else (wS2, B_wS2)

    def emit_attention(prefix):
        x_dram = d_x_pre if prefix else d_x_main
        for tile in range(NTILE):
            t0 = tile * TILE
            if prefix:
                dma("sync", x_pre_t[:, :, :],
                    x_dram.rearrange("(hc p) t -> p hc t", p=128)[:, :, t0:t0 + TILE]
                    .bitcast(F32R), reads=[B_otm], writes=[B_otm])
                xB = [B_otm]

                def xview(hc, c0, w, _t0=t0):
                    return x_pre_t[:, hc, c0 - _t0:c0 - _t0 + w]
            else:
                xB = [BxT[tile]]

                def xview(hc, c0, w):
                    return x_res[:, hc, c0:c0 + w]

            dma("sync", wSq[:, :, :],
                s_qkgtT.rearrange("(dc p) c -> p dc c", p=128),
                reads=[B_("wSdB"), B_scr["qkgt"]], writes=[B_("wSdB")])
            ts_(wSq[:, :, 0:KD], wSq[:, :, 0:KD], 0.125, None, ALU.mult, None,
                reads=[B_("wSdB")], writes=[B_("wSdB")])
            dma("sync", wig2s, s_ig2T, reads=[B_("wig2s"), B_scr["ig2"]],
                writes=[B_("wig2s")])

            emit_ln(xview, xB, TILE, t0, h_att, [B_hatt], 8, 0,
                    stats_bf16=prefix)

            if prefix and tile == 0:
                mset_(x_ext[:, :, 0:3], 0.0, reads=[B_xext], writes=[B_xext])
            else:
                cp_(x_ext[:, :, 0:3], tail_sb[:, :, :],
                    reads=[B_xext, B_("tail")], writes=[B_xext])
                if (not prefix) and tile == 0:
                    S.add("vector",
                          lambda e: e.tensor_scalar_mul(x_ext[:, :, 0:3],
                                                        x_ext[:, :, 0:3],
                                                        smask_sb[:, 0:1]),
                          reads=[B_xext, B_("smask")], writes=[B_xext],
                          aps=[(x_ext[:, :, 0:3], True), (smask_sb[:, 0:1], False)])

            def big_proj(scr, scrb, out_write):
                for g8 in range(8):
                    wbuf, Bw = attn_stream_buf()
                    dma("sync", wbuf[:, :, 0:256],
                        scr.rearrange("(hc p) d -> p hc d", p=128)
                        [:, :, g8 * 256:(g8 + 1) * 256], reads=[Bw, scrb],
                        writes=[Bw])
                    for d in range(2):
                        dc = g8 * 2 + d
                        slot = dc % 4
                        for hc in range(HC_N):
                            mm_(PS[slot][:, :], wbuf[:, hc, d * 128:(d + 1) * 128],
                                h_att[:, hc, :], start=(hc == 0),
                                stop=(hc == HC_N - 1),
                                reads=[Bw, B_hatt, PB[slot]], writes=[PB[slot]])
                        out_write(dc, slot)

            def up_out(dc, slot):
                cp_(x_ext[:, dc, 3:3 + 512], PS[slot][:, :],
                    reads=[PB[slot], B_xext], writes=[B_xext, PB[slot]],
                    eng="vector" if dc % 2 == 0 else "scalar")

            big_proj(s_upT, B_scr["up"], up_out)

            if not prefix:
                def fg_out(dc, slot):
                    cp_(sfg[:, dc, :], PS[slot][:, :],
                        reads=[PB[slot], B_sfg], writes=[B_sfg, PB[slot]],
                        eng="vector" if dc % 2 == 0 else "scalar")
                big_proj(s_gateT, B_scr["gate"], fg_out)

            for dc in range(DC_N):
                S.add("vector",
                      (lambda e, dc=dc: e.tensor_scalar_mul(
                          shift[:, dc, :], x_ext[:, dc, 0:512],
                          conv_w_sb[:, dc, 0:1])),
                      reads=[B_xext, B_("convw"), B_shift], writes=[B_shift],
                      aps=[(shift[:, dc, :], True), (x_ext[:, dc, 0:512], False),
                           (conv_w_sb[:, dc, 0:1], False)])
                for j in range(1, 4):
                    stt_(shift[:, dc, :], x_ext[:, dc, j:j + 512],
                         conv_w_sb[:, dc, j:j + 1], shift[:, dc, :],
                         ALU.mult, ALU.add,
                         reads=[B_xext, B_("convw"), B_shift], writes=[B_shift])
            cp_(tail_sb[:, :, :], x_ext[:, :, TILE:TILE + 3],
                reads=[B_xext, B_("tail")], writes=[B_("tail")])

            for dc in range(DC_N):
                S.add("scalar",
                      (lambda e, dc=dc: e.activation(
                          shift[:, dc, :], shift[:, dc, :], AF.Silu,
                          bias=conv_b_sb[:, dc:dc + 1])),
                      reads=[B_shift, B_("convb")], writes=[B_shift],
                      aps=[(shift[:, dc, :], True),
                           (conv_b_sb[:, dc:dc + 1], False)])
            if not prefix:
                for dc in range(DC_N):
                    act_(sfg[:, dc, :], sfg[:, dc, :], AF.Silu,
                         reads=[B_sfg], writes=[B_sfg])

            for dc in range(DC_N):
                mm_(PS[4][0:RI, :], wig1[:, dc, :], x_ext[:, dc, 3:3 + 512],
                    start=(dc == 0), stop=(dc == DC_N - 1),
                    reads=[B_("wig1"), B_xext, PB[4]], writes=[PB[4]])
            cp_(igz, PS[4][0:RI, :], reads=[PB[4], B_("kn")],
                writes=[B_("kn"), PB[4]])
            for dc in range(DC_N):
                slot = 4 + (dc % 2)
                mm_(PS[slot][:, :], wig2s[:, dc * 128:(dc + 1) * 128], igz,
                    True, True, reads=[B_("wig2s"), B_("kn"), PB[slot]],
                    writes=[PB[slot]])
                S.add("scalar",
                      (lambda e, dc=dc, slot=slot: e.activation(
                          vbuf[:, dc, :], PS[slot][:, :], AF.Sigmoid,
                          bias=ig_b2_sb[:, dc:dc + 1])),
                      reads=[PB[slot], B_("igb2"), B_v], writes=[B_v, PB[slot]],
                      aps=[(vbuf[:, dc, :], True), (PS[slot][:, :], False),
                           (ig_b2_sb[:, dc:dc + 1], False)])
                tt_(vbuf[:, dc, :], vbuf[:, dc, :], x_ext[:, dc, 3:3 + 512], ALU.mult,
                    reads=[B_v, B_xext], writes=[B_v])

            for half, slot in ((0, 6), (1, 7)):
                for dc in range(DC_N):
                    mm_(PS[slot][:, :], wSq[:, dc, half * 128:(half + 1) * 128],
                        shift[:, dc, :], start=(dc == 0), stop=(dc == DC_N - 1),
                        reads=[B_("wSdB"), B_shift, PB[slot]], writes=[PB[slot]])
            if not prefix:
                cp_(qs[:, :], PS[6][0:KD, :], reads=[PB[6]], writes=[B_("qs")])
            cp_(sqf[KD:128, :], PS[6][KD:128, :],
                reads=[PB[6], B_("sqf")], writes=[B_("sqf"), PB[6]])
            dma("sync", scr_k2[:, :], sqf.ap().bitcast(F32)[KD:128, :],
                reads=[B_("sqf"), B_("scrk2")], writes=[B_("scrk2")])
            act_(sqf[0:KD, 0:TILE], scr_k2[:, :], AF.Square,
                 reads=[B_("scrk2"), B_("sqf")], writes=[B_("sqf")])
            mm_(PS[4][0:1, :], ones_r[0:KD, :], sqf[0:KD, 0:TILE], True, True,
                reads=[B_("sqf"), B_("onesr"), PB[4]], writes=[PB[4]])
            act_(rowa, PS[4][0:1, :], AF.Ln, reads=[PB[4]],
                 writes=[B_("rowa"), PB[4]])
            act_(rowa, rowa.bitcast(F32), AF.Exp, scale=-0.5,
                 reads=[B_("rowa")], writes=[B_("rowa")])
            ts_(rowa, rowa.bitcast(F32), 1e12, None, ALU.min, None,
                reads=[B_("rowa")], writes=[B_("rowa")])
            mm_(PS[4][0:KD, :], ones_rows[0:1, 0:KD], rowa, True, True,
                reads=[B_("rowa"), B_("onesrw"), PB[4]], writes=[PB[4]])
            cp_(sqf[0:KD, :], PS[4][0:KD, :],
                reads=[PB[4], B_("sqf")], writes=[B_("sqf"), PB[4]])
            tt_(kn[:, :], scr_k2[:, :], sqf.ap().bitcast(F32)[0:KD, :], ALU.mult,
                reads=[B_("scrk2"), B_("sqf")], writes=[B_("kn")])
            act_(scr_k[:, :], PS[7][0:KD, :], AF.Abs, bias=g_b_sb[:, 0:1],
                 reads=[PB[7], B_("gb")], writes=[B_("scrk")])
            act_(scr_k[:, :], scr_k[:, :], AF.Exp, scale=-1.0,
                 reads=[B_("scrk")], writes=[B_("scrk")])
            act_(scr_k[:, :], scr_k[:, :], AF.Ln, bias=1.0,
                 reads=[B_("scrk")], writes=[B_("scrk")])
            act_(g_gate[:, :], PS[7][0:KD, :], AF.Relu, bias=g_b_sb[:, 0:1],
                 reads=[PB[7], B_("gb")], writes=[B_("gg")])
            tt_(g_gate[:, :], g_gate[:, :], scr_k[:, :], ALU.add,
                reads=[B_("gg"), B_("scrk")], writes=[B_("gg")])
            cp_(sqf[KD:128, :], PS[7][KD:128, :],
                reads=[PB[7], B_("sqf")], writes=[B_("sqf"), PB[7]])
            dma("sync", scr_k[:, :], sqf.ap().bitcast(F32)[KD:128, :],
                reads=[B_("sqf"), B_("scrk")], writes=[B_("scrk")])
            act_(scr_k[:, :], scr_k[:, :], AF.Exp, bias=ntau_b[:, 0:1],
                 scale=-1.0, reads=[B_("scrk"), B_("ntaub")], writes=[B_("scrk")])
            ts_(scr_k[:, :], scr_k[:, :], 1.0, None, ALU.add, None,
                reads=[B_("scrk")], writes=[B_("scrk")])
            recip_(scr_k[:, :], scr_k[:, :], reads=[B_("scrk")], writes=[B_("scrk")])
            stt_(scr_k2[:, :], g_gate[:, :], -1.0, scr_k[:, :], ALU.mult, ALU.mult,
                 reads=[B_("gg"), B_("scrk")], writes=[B_("scrk2")])
            scan_(bg[:, :], scr_k2[:, :],
                  cst_sb[0:KD, 7:8].broadcast_to([KD, TILE]),
                  reads=[B_("scrk2"), B_("consts")], writes=[B_("bg")])
            act_(g_gate[:, :], g_gate[:, :], AF.Ln, reads=[B_("gg")],
                 writes=[B_("gg")])
            tt_(g_gate[:, :], g_gate[:, :], scr_k[:, :], ALU.mult,
                reads=[B_("gg"), B_("scrk")], writes=[B_("gg")])
            act_(g_gate[:, :], g_gate[:, :], AF.Exp, reads=[B_("gg")],
                 writes=[B_("gg")])
            tt_(kn[:, :], kn[:, :], g_gate[:, :], ALU.mult,
                reads=[B_("kn"), B_("gg")], writes=[B_("kn")])

            for c in range(NCH):
                for g4 in range(4):
                    slot = 4 + (g4 % 2)
                    for d in range(4):
                        dc = g4 * 4 + d
                        tr_(PS[slot][:, d * 64:(d + 1) * 64].bitcast(BF16),
                            vbuf[:, dc, c * 128:(c + 1) * 128], ident_b[:],
                            reads=[B_v, B_("identb"), PB[slot]],
                            writes=[PB[slot]])
                    cp_(vt_tm[:, c, g4 * 512:(g4 + 1) * 512],
                        PS[slot][:, 0:256].bitcast(BF16),
                        reads=[PB[slot], B_xext], writes=[B_xext, PB[slot]],
                        eng="vector" if g4 % 2 == 0 else "scalar")

            for c in range(NCH):
                t0c = c * CHUNK
                Rcol = cst_sb[0:KD, 7:8] if c == 0 else bg[:, t0c - 1:t0c]
                ts_(scr_k[:, 0:CHUNK], bg[:, t0c:t0c + CHUNK], Rcol, None,
                    ALU.subtract, None, reads=[B_("bg")], writes=[B_("scrk")])
                act_(scr_k2[:, 0:CHUNK], scr_k[:, 0:CHUNK], AF.Exp,
                     reads=[B_("scrk")], writes=[B_("scrk2")])
                if not prefix:
                    tt_(qe2[:, :], qs[:, t0c:t0c + CHUNK], scr_k2[:, 0:CHUNK],
                        ALU.mult, reads=[B_("qs"), B_("scrk2")], writes=[B_("qe2")])
                ts_(scr_k2[:, 0:1], bg[:, t0c + CHUNK - 1:t0c + CHUNK], Rcol, None,
                    ALU.subtract, None, reads=[B_("bg")], writes=[B_("scrk2")])
                act_(ebtot, scr_k2[:, 0:1], AF.Exp,
                     reads=[B_("scrk2"), B_("gg")], writes=[B_("gg")])
                ts_(scr_k[:, 0:CHUNK], bg[:, t0c:t0c + CHUNK],
                    bg[:, t0c + CHUNK - 1:t0c + CHUNK], None, ALU.subtract, None,
                    reads=[B_("bg")], writes=[B_("scrk")])
                act_(scr_k[:, 0:CHUNK], scr_k[:, 0:CHUNK], AF.Exp, scale=-1.0,
                     reads=[B_("scrk")], writes=[B_("scrk")])
                tt_(kdec[:, :], kn[:, t0c:t0c + CHUNK], scr_k[:, 0:CHUNK], ALU.mult,
                    reads=[B_("kn"), B_("scrk")], writes=[B_("kdec")])
                tr_(PS[4][0:CHUNK, 0:KD // 2].bitcast(BF16), kdec[:, :],
                    ident_b[0:KD, 0:KD],
                    reads=[B_("kdec"), B_("identb"), PB[4]], writes=[PB[4]])
                cp_(kdec_tm[:, :], PS[4][0:CHUNK, 0:KD // 2].bitcast(BF16),
                    reads=[PB[4]], writes=[B_("kdtm"), PB[4]])

                if not prefix:
                    a_sb = a_t[c % 2]
                    B_at = B_("at%d" % (c % 2))
                    for m in range(4):
                        ts_(scr_k[:, 32 * m:32 * m + 32],
                            bg[:, t0c + 32 * m:t0c + 32 * m + 32],
                            bg[:, t0c + 32 * m:t0c + 32 * m + 1], None,
                            ALU.subtract, None,
                            reads=[B_("bg")], writes=[B_("scrk")])
                    act_(scr_k2[:, 0:CHUNK], scr_k[:, 0:CHUNK], AF.Exp,
                         reads=[B_("scrk")], writes=[B_("scrk2")])
                    tt_(qhat_d[:, :], qs[:, t0c:t0c + CHUNK], scr_k2[:, 0:CHUNK],
                        ALU.mult, reads=[B_("qs"), B_("scrk2")], writes=[B_("qhd")])
                    act_(scr_k2[:, 0:CHUNK], scr_k[:, 0:CHUNK], AF.Exp, scale=-1.0,
                         reads=[B_("scrk")], writes=[B_("scrk2")])
                    tt_(khat_d[:, :], kn[:, t0c:t0c + CHUNK], scr_k2[:, 0:CHUNK],
                        ALU.mult, reads=[B_("kn"), B_("scrk2")], writes=[B_("khd")])
                    c63 = bg[:, t0c + 63:t0c + 64]
                    ts_(scr_k[:, 0:64], bg[:, t0c + 64:t0c + 128], c63, None,
                        ALU.subtract, None, reads=[B_("bg")], writes=[B_("scrk")])
                    act_(scr_k[:, 0:64], scr_k[:, 0:64], AF.Exp,
                         reads=[B_("scrk")], writes=[B_("scrk")])
                    tt_(qhat_1[:, :], qs[:, t0c + 64:t0c + 128], scr_k[:, 0:64],
                        ALU.mult, reads=[B_("qs"), B_("scrk")], writes=[B_("qh1")])
                    ts_(scr_k[:, 0:64], bg[:, t0c:t0c + 64], c63, None,
                        ALU.subtract, None, reads=[B_("bg")], writes=[B_("scrk")])
                    act_(scr_k[:, 0:64], scr_k[:, 0:64], AF.Exp, scale=-1.0,
                         reads=[B_("scrk")], writes=[B_("scrk")])
                    tt_(khat_1[:, :], kn[:, t0c:t0c + 64], scr_k[:, 0:64], ALU.mult,
                        reads=[B_("kn"), B_("scrk")], writes=[B_("kh1")])
                    for li, (rq0, rk0, bc) in enumerate(((32, 0, 31), (96, 64, 95))):
                        cc2 = bg[:, t0c + bc:t0c + bc + 1]
                        ts_(scr_k[:, li * 32:li * 32 + 32],
                            bg[:, t0c + rq0:t0c + rq0 + 32], cc2, None,
                            ALU.subtract, None, reads=[B_("bg")],
                            writes=[B_("scrk")])
                        ts_(scr_k[:, 64 + li * 32:64 + li * 32 + 32],
                            bg[:, t0c + rk0:t0c + rk0 + 32], cc2, None,
                            ALU.subtract, None, reads=[B_("bg")],
                            writes=[B_("scrk")])
                    act_(scr_k2[:, 0:64], scr_k[:, 0:64], AF.Exp,
                         reads=[B_("scrk")], writes=[B_("scrk2")])
                    act_(scr_k2[:, 64:128], scr_k[:, 64:128], AF.Exp, scale=-1.0,
                         reads=[B_("scrk")], writes=[B_("scrk2")])
                    tt_(qhat_2[:, 32:64], qs[:, t0c + 32:t0c + 64], scr_k2[:, 0:32],
                        ALU.mult, reads=[B_("qs"), B_("scrk2")], writes=[B_("qh2")])
                    tt_(qhat_2[:, 96:128], qs[:, t0c + 96:t0c + 128],
                        scr_k2[:, 32:64], ALU.mult,
                        reads=[B_("qs"), B_("scrk2")], writes=[B_("qh2")])
                    tt_(khat_2[:, 0:32], kn[:, t0c:t0c + 32], scr_k2[:, 64:96],
                        ALU.mult, reads=[B_("kn"), B_("scrk2")], writes=[B_("kh2")])
                    tt_(khat_2[:, 64:96], kn[:, t0c + 64:t0c + 96],
                        scr_k2[:, 96:128], ALU.mult,
                        reads=[B_("kn"), B_("scrk2")], writes=[B_("kh2")])
                    mm_(PS[0][:, 0:128], khat_d[:, :], qhat_d[:, :], True, True,
                        reads=[B_("khd"), B_("qhd"), PB[0]], writes=[PB[0]])
                    mm_(PS[1][0:64, 0:64], khat_1[:, :], qhat_1[:, :], True, True,
                        reads=[B_("kh1"), B_("qh1"), PB[1]], writes=[PB[1]])
                    mm_(PS[2][:, 0:128], khat_2[:, :], qhat_2[:, :], True, True,
                        reads=[B_("kh2"), B_("qh2"), PB[2]], writes=[PB[2]])
                    for m in range(4):
                        tt_(a_sb[32 * m:32 * m + 32, 32 * m:32 * m + 32],
                            PS[0][32 * m:32 * m + 32, 32 * m:32 * m + 32],
                            mask32[:, :], ALU.mult,
                            reads=[PB[0], B_("mask32"), B_at], writes=[B_at])
                    cp_(a_sb[0:64, 64:128], PS[1][0:64, 0:64],
                        reads=[PB[1], B_at], writes=[B_at, PB[1]])
                    cp_(a_sb[0:32, 32:64], PS[2][0:32, 32:64],
                        reads=[PB[2], B_at], writes=[B_at])
                    cp_(a_sb[64:96, 96:128], PS[2][64:96, 96:128],
                        reads=[PB[2], B_at], writes=[B_at, PB[2]])
                    cp_(s_bf[:, :], s_state[:, :], reads=[B_("S")],
                        writes=[B_("sbf")], eng="scalar")
                    for q in range(4):
                        vc0 = q * 512
                        oslot = 4 + (q % 2)
                        mm_(PS[oslot][:, :], qe2[:, :], s_bf[:, vc0:vc0 + 512],
                            True, False,
                            reads=[B_("qe2"), B_("sbf"), PB[oslot]],
                            writes=[PB[oslot]])
                        mm_(PS[oslot][:, :], a_sb[:, :], vt_tm[:, c, vc0:vc0 + 512],
                            False, True,
                            reads=[B_at, B_xext, PB[oslot]], writes=[PB[oslot]])
                        cp_(o_tm[:, c, vc0:vc0 + 512], PS[oslot][:, :],
                            reads=[PB[oslot], B_otm], writes=[B_otm, PB[oslot]],
                            eng="vector" if q % 2 == 0 else "scalar")
                for q in range(4):
                    vc0 = q * 512
                    uslot = 6 + (q % 2)
                    mm_(PS[uslot][0:KD, :], kdec_tm[:, :], vt_tm[:, c, vc0:vc0 + 512],
                        True, True,
                        reads=[B_("kdtm"), B_xext, PB[uslot]], writes=[PB[uslot]])
                    stt_(s_state[:, vc0:vc0 + 512], s_state[:, vc0:vc0 + 512],
                         ebtot, PS[uslot][0:KD, :], ALU.mult, ALU.add,
                         reads=[B_("S"), B_("gg"), PB[uslot]],
                         writes=[B_("S"), PB[uslot]])

            if not prefix:
                for dc in range(DC_N):
                    slot = 4 + (dc % 2)
                    for cc in range(NCH):
                        tr_(PS[slot][:, cc * 64:(cc + 1) * 64].bitcast(BF16),
                            o_tm[:, cc, dc * 128:(dc + 1) * 128], ident_b[:],
                            reads=[B_otm, B_("identb"), PB[slot]],
                            writes=[PB[slot]])
                    stt_(vbuf[:, dc, :], shift[:, dc, :], rw_sb[:, dc:dc + 1],
                         PS[slot][:, 0:256].bitcast(BF16), ALU.mult, ALU.add,
                         reads=[B_shift, B_("rw"), PB[slot], B_v],
                         writes=[B_v, PB[slot]])
                    tt_(vbuf[:, dc, :], vbuf[:, dc, :], sfg[:, dc, :], ALU.mult,
                        reads=[B_v, B_sfg], writes=[B_v])
                for dc in range(DC_N):
                    act_(sqf[:, 0:512], vbuf[:, dc, :], AF.Square,
                         reads=[B_v, B_("sqf")], writes=[B_("sqf")])
                    mm_(PS[6][0:1, :], ones_r[:], sqf[:, 0:512], start=(dc == 0),
                        stop=(dc == DC_N - 1),
                        reads=[B_("sqf"), B_("onesr"), PB[6]], writes=[PB[6]])
                act_(rowa, PS[6][0:1, :], AF.Ln, bias=EPS, scale=1.0 / DI,
                     reads=[PB[6]], writes=[B_("rowa"), PB[6]])
                act_(rowa, rowa.bitcast(F32), AF.Exp, scale=-0.5,
                     reads=[B_("rowa")], writes=[B_("rowa")])
                mm_(PS[6][:, :], ones_rows[0:1, :], rowa, True, True,
                    reads=[B_("rowa"), B_("onesrw"), PB[6]], writes=[PB[6]])
                for dc in range(DC_N):
                    tt_(shift[:, dc, :], vbuf[:, dc, :], PS[6][:, :], ALU.mult,
                        reads=[B_v, PB[6], B_shift],
                        writes=([B_shift, PB[6]] if dc == DC_N - 1 else [B_shift]))
                for pr in range(4):
                    wv = wSdA if pr % 2 == 0 else wSdB
                    Bw = [B_wS, B_wS2] if pr % 2 == 0 else [B_("wSdB")]
                    dma("sync", wv[:, :, :],
                        s_downT.rearrange("(dc p) h -> p dc h", p=128)
                        [:, :, pr * 256:(pr + 1) * 256],
                        reads=Bw + [B_scr["down"]], writes=Bw)
                    for h2 in range(2):
                        hc = pr * 2 + h2
                        slot = hc % 2
                        for dc in range(DC_N):
                            mm_(PS[slot][:, :],
                                wv[:, dc, h2 * 128:(h2 + 1) * 128],
                                shift[:, dc, :], start=(dc == 0),
                                stop=(dc == DC_N - 1),
                                reads=Bw + [B_shift, PB[slot]], writes=[PB[slot]])
                        stt_(x_res[:, hc, t0:t0 + 512], PS[slot][:, :],
                             modT[:, 16 + hc:17 + hc],
                             x_res[:, hc, t0:t0 + 512].bitcast(F32),
                             ALU.mult, ALU.add,
                             reads=[PB[slot], B_("modT"), BxT[tile]],
                             writes=[BxT[tile], PB[slot]])

    emit_attention(prefix=True)
    S.add("vector", lambda e: e.tensor_scalar_mul(s_state[:, :], s_state[:, :],
                                                  smask_sb[0:KD, 0:1]),
          reads=[B_("S"), B_("smask")], writes=[B_("S")],
          aps=[(s_state[:, :], True), (smask_sb[0:KD, 0:1], False)])
    emit_attention(prefix=False)

    if "x1" in DEBUG_TAPS:
        d_dbg = nc.dram_tensor("dbg_x1", [H, TC], F32, kind="ExternalOutput").ap()
        dma("gpsimd", d_dbg.rearrange("(hc p) t -> p hc t", p=128),
            x_res.ap().bitcast(F32), reads=BxT, writes=[B_("dbgd")])
    if "epi" in DEBUG_TAPS:
        for nm, ap_, cols in [("sh", shift.rearrange("p a b -> p (a b)").bitcast(F32), 4096),
                              ("fg", sfg.rearrange("p a b -> p (a b)").bitcast(F32), 4096),
                              ("ot", o_tm.rearrange("p a b -> p (a b)").bitcast(F32), 4096),
                              ("xg", vbuf.rearrange("p a b -> p (a b)").bitcast(F32), 4096),
                              ("vt", vt_tm.rearrange("p a b -> p (a b)").bitcast(F32), 4096),
                              ("ss", s_state[:, :], 2048),
                              ("qs2", qs.ap().bitcast(F32)[:, :], 256),
                              ("kn2", kn.ap().bitcast(F32)[:, :], 256),
                              ("gg2", g_gate[:, :], 512),
                              ("bg2", bg[:, :], 512)]:
            dd = nc.dram_tensor("dbg_" + nm, [ap_.shape[0], cols], F32,
                                kind="ExternalOutput").ap()
            dma("gpsimd", dd, ap_,
                reads=[B_shift, B_sfg, B_otm, B_v, B_xext, B_("S"), B_("qs"),
                       B_("kn"), B_("gg"), B_("bg")],
                writes=[B_("dbge")])
    if "prep" in DEBUG_TAPS:
        d_dbg2 = nc.dram_tensor("dbg_prep", [6 * KD, TILE], F32,
                                kind="ExternalOutput").ap()
        for i_, (nm, ap_) in enumerate([("qs", qs.ap().bitcast(F32)[:, 0:256]),
                                        ("kn", kn.ap().bitcast(F32)[:, 0:256]),
                                        ("gg", g_gate[:, :]),
                                        ("bg", bg[:, :]),
                                        ("sk", scr_k[:, :]),
                                        ("sk2", scr_k2[:, :])]):
            dma("gpsimd", d_dbg2[i_ * KD:(i_ + 1) * KD, 0:ap_.shape[-1]], ap_,
                reads=[B_("qs"), B_("kn"), B_("gg"), B_("bg"), B_("scrk"),
                       B_("scrk2")], writes=[B_("dbgp")])

    B_hffn, B_affn, B_cb, B_dt, B_cfm = (B_("hffn"), B_("affn"), B_("cbBv"),
                                         B_("dtv"), B_("cfm"))
    for ft in range(NFT):
        t0 = ft * FT

        def xview_f(hc, c0, w):
            return x_res[:, hc, c0:c0 + w]

        dma("gpsimd", fgate_sb[:, :, :], d_fgate,
            reads=[B_("fgate")], writes=[B_("fgate"), B_otm, B_v])
        BxF = [BxT[2 * ft], BxT[2 * ft + 1]]
        emit_ln(xview_f, BxF, FT, t0, h_ffn, [B_hffn] + AL_hffn, 32, 24)

        for tc in range(FT // 128):
            for hc in range(HC_N):
                mm_(PS[4][0:128, 0:NE], h_ffn[:, hc, tc * 128:(tc + 1) * 128],
                    fgate_sb[:, hc, :], start=(hc == 0), stop=(hc == HC_N - 1),
                    reads=[B_hffn, B_("fgate"), PB[4]], writes=[PB[4]])
            cp_(tiny_a[:, :], PS[4][0:128, 0:NE], reads=[PB[4]],
                writes=[B_("tinya"), PB[4]])
            red_(m1c, tiny_a[:, :], ALU.max, reads=[B_("tinya")],
                 writes=[B_("m1")])
            red_(nm1c, tiny_a[:, :], ALU.max, negate=True,
                 reads=[B_("tinya")], writes=[B_("nm1")])
            ts_(tiny_b[:, :], tiny_a[:, :], m1c, -1e30, ALU.is_equal,
                ALU.mult, reads=[B_("tinya"), B_("m1")], writes=[B_("tinyb")])
            tt_(tiny_b[:, :], tiny_b[:, :], tiny_a[:, :], ALU.add,
                reads=[B_("tinyb"), B_("tinya")], writes=[B_("tinyb")])
            red_(m2c, tiny_b[:, :], ALU.max, reads=[B_("tinyb")],
                 writes=[B_("m2")])
            tt_(denc, m2c, nm1c, ALU.add,
                reads=[B_("m2"), B_("nm1")], writes=[B_("den")])
            act_(denc, denc, AF.Exp, reads=[B_("den")], writes=[B_("den")])
            ts_(denc, denc, 1.0, None, ALU.add, None,
                reads=[B_("den")], writes=[B_("den")])
            recip_(denc, denc, reads=[B_("den")], writes=[B_("den")])
            act_(c_tm[:, :], tiny_a[:, :], AF.Exp, bias=nm1c,
                 reads=[B_("tinya"), B_("nm1")], writes=[B_("ctm")])
            ts_(tiny_b[:, :], tiny_a[:, :], m2c, None, ALU.is_ge, None,
                reads=[B_("tinya"), B_("m2")], writes=[B_("tinyb")])
            stt_(c_tm[:, :], c_tm[:, :], denc, tiny_b[:, :],
                 ALU.mult, ALU.mult,
                 reads=[B_("ctm"), B_("den"), B_("tinyb")], writes=[B_("ctm")])
            tr_(PS[5][0:NE, 0:128], c_tm[:, :], ident_f[:],
                reads=[B_("ctm"), B_("identf"), PB[5]], writes=[PB[5]])
            cp_(c_fm[:, tc * 128:(tc + 1) * 128], PS[5][0:NE, 0:128],
                reads=[PB[5]], writes=[B_cfm, PB[5]] + AL_cfm)

        for ei in range(NE + 1):
            shared = (ei == NE)
            icn = ISC_N if shared else IC_N
            src_g = d_Sg if shared else d_Wg[ei]
            src_u = d_Su if shared else d_Wu[ei]
            src_d = d_Sd if shared else d_Wd[ei]
            if not shared:
                dma("sync", crow, c_fm[ei:ei + 1, :],
                    reads=[B_cfm, B_("crow")] +
                    ([B_v] if ei == 0 and ft == 0 else []),
                    writes=[B_("crow")])
                for q in range(2):
                    mm_(PS[6 + q][:, :], ones_rb[0:1, :],
                        crow[0:1, q * 512:(q + 1) * 512],
                        True, True,
                        reads=[B_("crow"), B_("onesrw"), PB[6 + q]],
                        writes=[PB[6 + q]])
                    cp_(cbB[:, q * 512:(q + 1) * 512], PS[6 + q][:, :],
                        reads=[PB[6 + q]], writes=[B_cb, PB[6 + q]] + AL_cb)
            for ic in range(icn):
                sl = ic % 2
                Bg, Bu = B_("wgS%d" % sl), B_("wuS%d" % sl)
                dma("gpsimd", wgS[:, sl, :, :],
                    src_g.rearrange("(hc p) i -> p hc i", p=128)
                    [:, :, ic * 128:(ic + 1) * 128],
                    reads=[Bg],
                    writes=[Bg] + (AL_wg if ic < 2 and ei == 0 and ft == 0
                                   else []))
                dma("gpsimd", wuS[:, sl, :, :],
                    src_u.rearrange("(hc p) i -> p hc i", p=128)
                    [:, :, ic * 128:(ic + 1) * 128],
                    reads=[Bu],
                    writes=[Bu] + (AL_wg if ic < 2 and ei == 0 and ft == 0
                                   else []))
                gb0 = 0 if ic % 2 == 0 else 4
                ub0 = 2 if ic % 2 == 0 else 6
                for hc in range(HC_N):
                    for tq in range(2):
                        mm_(PS[gb0 + tq][:, :], wgS[:, sl, hc, :],
                            h_ffn[:, hc, tq * 512:(tq + 1) * 512],
                            start=(hc == 0), stop=(hc == HC_N - 1),
                            reads=[Bg, B_hffn, PB[gb0 + tq]],
                            writes=[PB[gb0 + tq]])
                    for tq in range(2):
                        mm_(PS[ub0 + tq][:, :], wuS[:, sl, hc, :],
                            h_ffn[:, hc, tq * 512:(tq + 1) * 512],
                            start=(hc == 0), stop=(hc == HC_N - 1),
                            reads=[Bu, B_hffn, PB[ub0 + tq]],
                            writes=[PB[ub0 + tq]])
                for tq in range(2):
                    S.add("scalar",
                          (lambda e, ic=ic, tq=tq, gb0=gb0: e.activation(
                              a_ffn[:, ic, tq * 512:(tq + 1) * 512],
                              PS[gb0 + tq][:, :], AF.Silu)),
                          reads=[PB[gb0 + tq], B_affn],
                          writes=[B_affn, PB[gb0 + tq]] +
                          (AL_affn if ic < 2 and ei == 0 and ft == 0 else []),
                          aps=[(a_ffn[:, ic, tq * 512:(tq + 1) * 512], True),
                               (PS[gb0 + tq][:, :], False)])
                    tt_(a_ffn[:, ic, tq * 512:(tq + 1) * 512],
                        a_ffn[:, ic, tq * 512:(tq + 1) * 512],
                        PS[ub0 + tq][:, :], ALU.mult,
                        reads=[B_affn, PB[ub0 + tq]],
                        writes=[B_affn, PB[ub0 + tq]])
            h1 = (icn + 1) // 2
            for hc in range(HC_N):
                sl0 = (2 * hc) % 3
                sl1 = (2 * hc + 1) % 3
                Bd0, Bd1 = B_("wdS%d" % sl0), B_("wdS%d" % sl1)
                dma("gpsimd", wdS[:, sl0, 0:h1, :],
                    src_d.rearrange("(ic p) h -> p ic h", p=128)
                    [:, 0:h1, hc * 128:(hc + 1) * 128],
                    reads=[Bd0],
                    writes=[Bd0] + ([B_("wig2s"), B_("tail")]
                                    if hc < 3 and ei == 0 and ft == 0 else []))
                dma("gpsimd", wdS[:, sl1, 0:icn - h1, :],
                    src_d.rearrange("(ic p) h -> p ic h", p=128)
                    [:, h1:icn, hc * 128:(hc + 1) * 128], reads=[Bd1], writes=[Bd1])
                pd0 = 4 if hc % 2 == 0 else 6
                for ic in range(icn):
                    sl = sl0 if ic < h1 else sl1
                    Bd = Bd0 if ic < h1 else Bd1
                    icl = ic if ic < h1 else ic - h1
                    for tq in range(2):
                        mm_(PS[pd0 + tq][:, :], wdS[:, sl, icl, :],
                            a_ffn[:, ic, tq * 512:(tq + 1) * 512],
                            start=(ic == 0), stop=(ic == icn - 1),
                            reads=[Bd, B_affn, PB[pd0 + tq]],
                            writes=[PB[pd0 + tq]])
                for tq in range(2):
                    xsl = x_res[:, hc, t0 + tq * 512:t0 + (tq + 1) * 512]
                    Bxq = BxT[2 * ft + tq]
                    if shared:
                        stt_(xsl, PS[pd0 + tq][:, :], modT[:, 40 + hc:41 + hc],
                             xsl.bitcast(F32), ALU.mult, ALU.add,
                             reads=[PB[pd0 + tq], B_("modT"), Bxq],
                             writes=[Bxq, PB[pd0 + tq]])
                    else:
                        tt_(dt_scr[:, tq * 512:(tq + 1) * 512], PS[pd0 + tq][:, :],
                            cbB[:, tq * 512:(tq + 1) * 512], ALU.mult,
                            reads=[PB[pd0 + tq], B_cb, B_dt],
                            writes=[B_dt, PB[pd0 + tq]] +
                            (AL_cb if ei == 0 and ft == 0 and hc == 0 else []))
                        stt_(xsl, dt_scr[:, tq * 512:(tq + 1) * 512],
                             modT[:, 40 + hc:41 + hc], xsl.bitcast(F32),
                             ALU.mult, ALU.add,
                             reads=[B_dt, B_("modT"), Bxq], writes=[Bxq])

    dma("gpsimd", d_out.rearrange("(hc p) t -> p hc t", p=128),
        x_res.ap().bitcast(F32), reads=BxT, writes=[B_("outd")])
    S.add("gpsimd", lambda e: e.nop(), reads=[B_("outd")], writes=[], aps=[])

    S.finalize()
    sem_names = ["sync", "scalar", "vector", "tensor", "gpsimd",
                 "d_sync", "d_scalar", "d_gpsimd"]
    sems = {}
    for sn in sem_names:
        g = nc.semaphore("sem_" + sn)
        sems[sn] = g.__enter__()
        ctx.append(g)
    with nc.Block() as block:
        @block.sync
        def _(eng):
            S.emit("sync", eng, sems)

        @block.scalar
        def _(eng):
            S.emit("scalar", eng, sems)

        @block.vector
        def _(eng):
            S.emit("vector", eng, sems)

        @block.tensor
        def _(eng):
            S.emit("tensor", eng, sems)

        @block.gpsimd
        def _(eng):
            S.emit("gpsimd", eng, sems)

    for g in reversed(ctx):
        g.__exit__(None, None, None)
    return nc


_NC_CACHE = {}


def _get_nc():
    if "nc" not in _NC_CACHE:
        _NC_CACHE["nc"] = build_program()
    return _NC_CACHE["nc"]


def kernel(**inputs):
    hs = np.asarray(inputs["L0_attn_hidden_states"], dtype=np.float32)

    def pf(name):
        return np.ascontiguousarray(np.asarray(inputs[name], dtype=np.float32))

    common = dict(
        up_w=pf("L0_attn_up_w"),
        gate_w=pf("L0_attn_gate_w"),
        conv_w=np.ascontiguousarray(
            pf("L0_attn_conv_w").reshape(DC_N, 128, 4).transpose(1, 0, 2)),
        conv_b=np.ascontiguousarray(pf("L0_attn_conv_b").reshape(DC_N, 128).T),
        qkgt_w=np.ascontiguousarray(np.concatenate(
            [pf("L0_attn_q_w"), pf("L0_attn_k_w"),
             pf("L0_attn_g_w"), pf("L0_attn_tau_w")], axis=0)),
        g_b=np.ascontiguousarray(pf("L0_attn_g_b").reshape(KD, 1)),
        tau_b=np.ascontiguousarray(pf("L0_attn_tau_b").reshape(KD, 1)),
        ig_w1=pf("L0_attn_ig_w1"),
        ig_w2=pf("L0_attn_ig_w2"),
        ig_b2=np.ascontiguousarray(pf("L0_attn_ig_b2").reshape(DC_N, 128).T),
        rw=np.ascontiguousarray(
            pf("L0_attn_residual_weight").reshape(DC_N, 128).T),
        nw=np.ascontiguousarray(pf("L0_attn_norm_weight").reshape(DC_N, 128).T),
        down_w=pf("L0_attn_down_w"),
        fgate=np.ascontiguousarray(
            pf("L0_ffn_gate_w").reshape(HC_N, 128, NE).transpose(1, 0, 2)),
        Wg=pf("L0_ffn_Wg"), Wu=pf("L0_ffn_Wu"), Wd=pf("L0_ffn_Wd"),
        Sg=pf("L0_ffn_Sg"), Su=pf("L0_ffn_Su"), Sd=pf("L0_ffn_Sd"),
        wmod=pf("L0_wmod"),
    )
    c_all = pf("L0_c")

    in_maps = []
    for core in range(8):
        b, hf = core // 2, core % 2
        xm = np.ascontiguousarray(hs[b, hf * TC:(hf + 1) * TC, :].T)
        xp = np.ascontiguousarray(hs[b, 0:TC, :].T) if hf == 1 else xm
        m = dict(common)
        m.update(x_main=xm, x_pre=xp,
                 smask=np.full((128, 1), float(hf), np.float32),
                 cvec=np.ascontiguousarray(c_all[b].reshape(HC_N, 128).T))
        in_maps.append(m)

    nc = _get_nc()
    res = run_bass_kernel_spmd(nc, in_maps, core_ids=list(range(8)))
    out = np.empty((B, T, H), dtype=np.float32)
    for core in range(8):
        b, hf = core // 2, core % 2
        out[b, hf * TC:(hf + 1) * TC, :] = res.results[core]["out"].T
    return out
